# version 17
# baseline (speedup 1.0000x reference)
import os

import numpy as np

import concourse.bass as bass
import concourse.mybir as mybir
from concourse.bass_utils import run_bass_kernel_spmd

F32 = mybir.dt.float32
BF16 = mybir.dt.bfloat16
AF = mybir.ActivationFunctionType
ALU = mybir.AluOpType
AX = mybir.AxisListType

ENGS = ("tensor", "vector", "scalar", "gpsimd", "sync")

CFG_FULL = dict(
    NC=8, D=2048, TO=1024, TP=1024, SGT=512, PGT=256,
    H=16, DK=64, DV=128, M=64,
    E=16, F=1024, TOPK=4, SHF=5632, DSEG=512, SHGRP=8,
)

CFG_SMALL = dict(
    NC=2, D=256, TO=256, TP=256, SGT=256, PGT=256,
    H=2, DK=64, DV=128, M=64,
    E=4, F=256, TOPK=2, SHF=384, DSEG=256, SHGRP=2,
)


class Prog:

    def __init__(self, nc):
        self.nc = nc
        self.stream = {e: [] for e in ENGS}
        self.cnt = {}
        self.hi = {e: {} for e in ENGS}
        self.sems = {}

    def sem(self, name):
        self.cnt[name] = 0

    def _wait(self, eng, tok):
        if tok is None:
            return
        s, v, _ = tok
        if self.hi[eng].get(s, 0) >= v:
            return
        self.hi[eng][s] = v

        def w(E, s=s, v=v):
            E.wait_ge(self.sems[s], v)

        self.stream[eng].append(w)

    def op(self, eng, fn, waits=(), inc=None, amt=1):
        for t in waits:
            if t is None:
                continue
            if isinstance(t, tuple):
                self._wait(eng, t)
            else:
                for tt in t:
                    self._wait(eng, tt)
        tok = None
        if inc is not None:
            self.cnt[inc] += amt
            tok = (inc, self.cnt[inc], eng)

        def run(E, fn=fn, inc=inc, amt=amt):
            r = fn(E)
            if inc is not None:
                r.then_inc(self.sems[inc], amt)

        self.stream[eng].append(run)
        return tok

    def wait_only(self, eng, tok):
        self._wait(eng, tok)

    def barrier(self):
        toks = [(s, self.cnt[s], None) for s in ("mm", "ve", "ac", "gp")]
        for e in ("tensor", "vector", "scalar", "gpsimd"):
            for t in toks:
                self._wait(e, t)


class Pool:

    def __init__(self, banks):
        self.banks = banks
        self.free = [[] for _ in banks]
        self.order = list(range(len(banks)))

    def acq(self):
        i = self.order.pop(0)
        return i, self.banks[i], list(self.free[i])

    def rel(self, i, tok):
        self.free[i] = tok if isinstance(tok, list) else [tok]
        self.order.append(i)


def build(cfg):
    D = cfg["D"]; DC = D // 128
    TO = cfg["TO"]; TP = cfg["TP"]
    SGT = cfg["SGT"]; PGT = cfg["PGT"]
    NSG = TO // SGT; NPG = TP // PGT
    CPS = SGT // 128
    PPS = PGT // 128
    H = cfg["H"]; DK = cfg["DK"]; DV = cfg["DV"]; M = cfg["M"]
    NEXP = cfg["E"]; F = cfg["F"]; FJ = F // 128
    TOPK = cfg["TOPK"]; SHF = cfg["SHF"]; SHJ = SHF // 128
    DSEG = cfg["DSEG"]; NDS = D // DSEG
    SHGRP = cfg["SHGRP"]
    KC = H * DK // 128
    VC = H * DV // 128
    SC = H * M // 128
    EPS = 1e-5
    SLD = min(4, DC)
    SLOT = max(D, 2 * DSEG, SLD * 512)
    assert DSEG <= SLOT and D % 128 == 0

    nc = bass.Bass("TRN2", target_bir_lowering=False, debug=False)

    def din(name, shape):
        return nc.dram_tensor(name, shape, F32, kind="ExternalInput").ap()

    xo = din("xo", [TO, D])
    xp = din("xp", [TP, D])
    wq = din("wq", [D, H * DK]); wk = din("wk", [D, H * DK])
    wv = din("wv", [D, H * DV]); ws = din("ws", [D, H * M])
    wg = din("wg", [D, H * DV]); wo = din("wo", [H * DV, D])
    wrt = din("wrt", [D, NEXP + 1])
    wgate = din("wgate", [NEXP, D, F]); wup = din("wup", [NEXP, D, F])
    wdown = din("wdown", [NEXP, F, D])
    wsg = din("wsg", [D, SHF]); wsu = din("wsu", [D, SHF])
    wsd = din("wsd", [SHF, D])
    identH = din("identH", [128, 128]); utriH = din("utriH", [128, 128])
    onesH = din("onesH", [128, 128]); gnwrH = din("gnwrH", [128, DV])
    flagH = din("flagH", [128, 1]); cvalsH = din("cvalsH", [128, 4])
    out = nc.dram_tensor("out", [TO, D], F32, kind="ExternalOutput").ap()

    xo_t = xo.rearrange("(n p) d -> n p d", p=128)
    xp_t = xp.rearrange("(n p) d -> n p d", p=128)
    out_t = out.rearrange("(n p) d -> n p d", p=128)
    wo_t = wo.rearrange("(v p) d -> v p d", p=128)

    def colslice(w, c0):
        return w.rearrange("(dc p) c -> p dc c", p=128)[:, :, c0:c0 + 128]

    def wd_slice(w3, ds, j0, jn):
        return w3.rearrange("(fj p) d -> p fj d", p=128)[:, j0:j0 + jn,
                                                         ds * DSEG:(ds + 1) * DSEG]

    def sb(name, shape, dt=F32):
        return nc.alloc_sbuf_tensor(name, shape, dt).ap()

    ident = sb("ident", [128, 128]); utri = sb("utri", [128, 128])
    onesb = sb("onesb", [128, 128]); gnwr = sb("gnwr", [128, DV])
    flag = sb("flag", [128, 1]); cvals = sb("cvals", [128, 4])
    nc.const_aps.register(F32, 0.0, cvals[:, 0:1])
    nc.const_aps.register(F32, -32.0, cvals[:, 1:2])
    nc.const_aps.register(F32, 1.0, cvals[:, 2:3])
    nc.const_aps.register(F32, 0.125, cvals[:, 3:4])
    wrt_hi = sb("wrt_hi", [128, DC, NEXP + 1], BF16)
    wrt_lo = sb("wrt_lo", [128, DC, NEXP + 1], BF16)

    nst = max(D // 512, 1)
    bnst = sb("bnst", [128, 6 * nst])
    mv2 = sb("mv2", [128, 2])
    ln_a = sb("ln_a", [128, 1]); ln_b = sb("ln_b", [128, 1])
    sm_m = sb("sm_m", [128, 1]); sm_s = sb("sm_s", [128, 1]); sm_r = sb("sm_r", [128, 1])
    rs1 = sb("rs1", [128, 1]); rs2 = sb("rs2", [128, 1]); rstd = sb("rstd", [128, 1])
    cm_m = [sb(f"cm_m{i}", [128, 1]) for i in range(2)]
    cm_s = [sb(f"cm_s{i}", [128, 1]) for i in range(2)]
    cm_r = [sb(f"cm_r{i}", [128, 1]) for i in range(2)]
    cr1 = [sb(f"cr1{i}", [128, 1]) for i in range(2)]
    cr2 = [sb(f"cr2{i}", [128, 1]) for i in range(2)]
    crs = [sb(f"crs{i}", [128, 1]) for i in range(2)]
    sqd = [sb(f"sqd{i}", [128, DV]) for i in range(2)]
    probs = sb("probs", [128, NEXP]); cur = sb("cur", [128, NEXP])
    sel = sb("sel", [128, NEXP]); eqb = sb("eqb", [128, NEXP]); w4 = sb("w4", [128, NEXP])
    per = sb("per", [128, NEXP])
    comb = sb("comb", [128, CPS, NEXP])
    shg = sb("shg", [128, CPS])
    silu_t = sb("silu_t", [128, SGT])
    af32 = silu_t

    U_all = sb("U_all", [128, max(H // 2, 1), DK])
    W_all = sb("W_all", [64, H, DV])
    A_all = sb("A_all", [1, H * M])

    res = sb("res", [128, CPS, D])
    xn_tm = sb("xn_tm", [128, min(D, 1024)])
    NTH = max(D // 1024, 1)

    NST = 3
    NSP = 6
    NWC = 4
    stg = [sb(f"stg{i}", [128, SLOT]) for i in range(NST)]
    spb = [sb(f"spb{i}", [128, 2 * SLOT], BF16) for i in range(NSP)]
    xr = [sb(f"xr{i}", [128, D]) for i in range(2)]
    use_wcache = os.environ.get("BASS_SIM_FP64") != "1"
    NCACHE = 768 if use_wcache else 0
    _wchs = [nc.dram_tensor(f"wch{i}", [128, 128, 2 * SLOT], BF16).ap()
             for i in range((NCACHE + 127) // 128)]

    def wch(idx):
        return _wchs[idx // 128][idx % 128]

    base0 = nc.sbuf_base

    def at(name, shape, dt, off):
        return nc.alloc_sbuf_tensor_at(name, shape, dt,
                                       offset=base0 + off).ap()

    xnp_hi = at("xnp_hi", [128, DC, PGT], BF16, 0)
    xnp_lo = at("xnp_lo", [128, DC, PGT], BF16, 2 * DC * PGT)
    o_ = 4 * DC * PGT
    Kp = at("Kp", [128, PPS, H * DK], F32, o_)
    Ep = at("Ep", [128, PPS, H * M], F32, o_ + 4 * PPS * H * DK)
    Vp = at("Vp", [128, PPS, H * DV], F32, o_ + 4 * PPS * H * (DK + M))
    n_pre = o_ + 4 * PPS * H * (DK + M + DV)
    xn_hi = at("xn_hi", [128, DC, SGT], BF16, 0)
    xn_lo = at("xn_lo", [128, DC, SGT], BF16, 2 * DC * SGT)
    act_hi = at("act_hi", [128, max(FJ, SHGRP), SGT], BF16, 4 * DC * SGT)
    act_lo = at("act_lo", [128, max(FJ, SHGRP), SGT], BF16,
                4 * DC * SGT + 2 * max(FJ, SHGRP) * SGT)
    n_ffn = 4 * DC * SGT + 4 * max(FJ, SHGRP) * SGT
    o_ = 0

    def acc_(n):
        nonlocal o_
        r = o_
        o_ += n
        return r

    Vc2 = [at("Vc0", [128, H * DV], F32, acc_(4 * H * DV))]
    SGc2 = [at("SGc0", [128, H * DV], F32, acc_(4 * H * DV))]
    Ec2 = [at("Ec0", [128, H * M], F32, acc_(4 * H * M))]
    Ktm2 = [at("Ktm0", [128, H * DK], F32, acc_(4 * H * DK))]
    xnc_hi2 = [at("xnc_hi0", [128, DC, 128], BF16, acc_(2 * DC * 128))]
    xnc_lo2 = [at("xnc_lo0", [128, DC, 128], BF16, acc_(2 * DC * 128))]

    Q_tm = at("Q_tm", [128, H * DK], F32, acc_(4 * H * DK))
    Qfm = at("Qfm", [128, KC, 128], F32, acc_(4 * H * DK))
    Kfm = at("Kfm", [128, KC, 128], F32, acc_(4 * H * DK))
    AmaskT2 = [at(f"AmaskT{i}", [128, 128], F32, acc_(512)) for i in range(2)]
    DT2 = [at(f"DT{i}", [128, 128], F32, acc_(512)) for i in range(2)]
    PRT2 = [at(f"PRT{i}", [128, 128], F32, acc_(512)) for i in range(2)]
    EcT2 = [at(f"EcT{i}", [128, 128], F32, acc_(512)) for i in range(2)]
    R2 = [at(f"R{i}", [128, M], F32, acc_(4 * M)) for i in range(2)]
    ok2 = [at(f"ok{i}", [128, M], F32, acc_(4 * M)) for i in range(2)]
    pe2 = [at(f"pe{i}", [128, M], F32, acc_(4 * M)) for i in range(2)]
    PR2 = [at(f"PR{i}", [128, M], F32, acc_(4 * M)) for i in range(2)]
    o12 = [at(f"o1{i}", [128, DV], F32, acc_(4 * DV)) for i in range(2)]
    of2 = [at(f"of{i}", [128, DV], F32, acc_(4 * DV)) for i in range(2)]
    ofm_hi = at("ofm_hi", [128, H, DV], BF16, acc_(2 * H * DV))
    ofm_lo = at("ofm_lo", [128, H, DV], BF16, acc_(2 * H * DV))
    n_att = o_
    Kfm_flat = Kfm.rearrange("p a b -> p (a b)")
    Vc2.append(Vc2[0])
    SGc2.append(SGc2[0])
    Ec2.append(Ec2[0])
    Ktm2.append(Ktm2[0])
    xnc_hi2.append(xnc_hi2[0])
    xnc_lo2.append(xnc_lo2[0])
    n_arena = max(n_pre, n_ffn, n_att)
    assert base0 + n_arena <= 229344, (base0, n_arena)

    pbank = [nc.alloc_psum_tensor(f"pb{i}", [128, 512], F32).ap()
             for i in range(8)]

    P = Prog(nc)
    SEMS = (["mm", "ve", "ac", "gp", "cs"]
            + [f"st{i}" for i in range(NST)]
            + [f"sw{i}" for i in range(NSP)]
            + [f"wc{i}" for i in range(NWC)]
            + ["gx0", "gx1"]
            + [f"so{i}" for i in range(CPS)])
    for s in SEMS:
        P.sem(s)
    pool = Pool(pbank)

    def T(fn, waits=(), inc="mm", skip_chk=False):
        return P.op("tensor", fn, waits, inc)

    def V(fn, waits=(), inc="ve"):
        return P.op("vector", fn, list(waits) + [("ve", P.cnt["ve"], None)], inc)

    def S(fn, waits=(), inc="ac"):
        return P.op("scalar", fn, list(waits) + [("ac", P.cnt["ac"], None)], inc)

    def G(fn, waits=(), inc="gp"):
        return P.op("gpsimd", fn, list(waits) + [("gp", P.cnt["gp"], None)], inc)

    def now(sem, eng=None):
        return (sem, P.cnt[sem], eng)

    class CachedSplitStream:

        def __init__(self):
            self.sti = 0
            self.spi = 0
            self.wci = 0
            self.st_free = [[] for _ in range(NST)]
            self.sp_free = [[] for _ in range(NSP)]
            self.slot_extra = [[] for _ in range(NSP)]
            self.wc_prev = [None] * NWC
            self.cache = {}
            self.nent = 0

        require_cached = False

        def load(self, key, hbm_ap, n):
            assert not self.require_cached or (key is not None
                                               and key in self.cache), key
            di = self.spi % NSP
            self.spi += 1
            slot = spb[di]
            hi = slot[:, 0:n]
            lo = slot[:, SLOT:SLOT + n]
            pair = spb[di].rearrange("p (a b) -> p a b", a=2)[:, :, 0:n]
            if key is not None and key in self.cache:
                idx, ctok = self.cache[key]
                cpair = wch(idx).rearrange("p (a b) -> p a b", a=2)[:, :, 0:n]
                dtok = P.op("sync",
                            lambda E, d=pair, s=cpair:
                            E.dma_start(out=d, in_=s),
                            waits=[self.sp_free[di], ctok],
                            inc=f"sw{di}", amt=16)
                self.sp_free[di] = []
                self.slot_extra[di] = []
                return di, hi, lo, [dtok]
            si = self.sti % NST
            self.sti += 1
            dtok = P.op("sync",
                        lambda E, d=stg[si][:, 0:n], s=hbm_ap:
                        E.dma_start(out=d, in_=s),
                        waits=[self.st_free[si]], inc=f"st{si}", amt=16)
            htok = S(lambda E, h=hi, s=stg[si][:, 0:n]: E.copy(h, s),
                     waits=[dtok] + self.sp_free[di])
            ltok = V(lambda E, l=lo, s=stg[si][:, 0:n], h=hi:
                     E.tensor_tensor(l, s, h, ALU.subtract),
                     waits=[dtok, htok] + self.sp_free[di])
            self.sp_free[di] = []
            self.st_free[si] = [ltok]
            toks = [htok, ltok]
            self.slot_extra[di] = []
            if key is not None and self.nent < NCACHE:
                idx = self.nent
                self.nent += 1
                k2 = self.wci % NWC
                self.wci += 1
                cpair = wch(idx).rearrange("p (a b) -> p a b", a=2)[:, :, 0:n]
                stok = P.op("gpsimd",
                            lambda E, d=cpair, s=pair:
                            E.dma_start(out=d, in_=s),
                            waits=toks + [self.wc_prev[k2]],
                            inc=f"wc{k2}", amt=16)
                self.wc_prev[k2] = stok
                self.cache[key] = (idx, stok)
                self.slot_extra[di] = [stok]
            return di, hi, lo, toks

        def rel(self, di, tok):
            tok = tok if isinstance(tok, list) else [tok]
            self.sp_free[di] = tok + self.slot_extra[di]

    s_w = CachedSplitStream()

    class XStream:
        def __init__(self):
            self.i = 0
            self.free = [[] for _ in range(2)]

        def load(self, hbm_ap):
            i = self.i % 2
            self.i += 1
            tok = P.op("gpsimd",
                       lambda E, d=xr[i], s=hbm_ap: E.dma_start(out=d, in_=s),
                       waits=[self.free[i]], inc=f"gx{i}", amt=16)
            return i, xr[i], tok

        def rel(self, i, tok):
            self.free[i] = tok if isinstance(tok, list) else [tok]

    s_x = XStream()

    def mm3(bank_ap, lh, ll, rh, rl, start, stop, waits=()):
        P.op("tensor", lambda E: E.matmul(bank_ap, lh, rh, start=start,
                                          stop=False), waits, None)
        P.op("tensor", lambda E: E.matmul(bank_ap, lh, rl, start=False,
                                          stop=False), (), None)
        return T(lambda E: E.matmul(bank_ap, ll, rh, start=False, stop=stop))

    for dst, src in ((ident, identH), (utri, utriH), (onesb, onesH),
                     (gnwr, gnwrH), (flag, flagH), (cvals, cvalsH)):
        P.op("gpsimd", lambda E, d=dst, s=src: E.dma_start(out=d, in_=s),
             inc="cs", amt=16)
    const_tok = P.op("gpsimd",
                     lambda E: E.dma_start(
                         out=stg[0][:, 0:DC * (NEXP + 1)],
                         in_=wrt.rearrange("(dc p) c -> p dc c", p=128)),
                     inc="cs", amt=16)
    for e in ENGS:
        P.wait_only(e, const_tok)
    wrt_hi_f = wrt_hi.rearrange("p a b -> p (a b)")
    wrt_lo_f = wrt_lo.rearrange("p a b -> p (a b)")
    h0 = S(lambda E: E.copy(wrt_hi_f, stg[0][:, 0:DC * (NEXP + 1)]),
           waits=[const_tok])
    wrt_tok = V(lambda E: E.tensor_tensor(wrt_lo_f,
                                          stg[0][:, 0:DC * (NEXP + 1)],
                                          wrt_hi_f, ALU.subtract), waits=[h0])
    s_w.st_free[0] = [wrt_tok]

    for a in (U_all.rearrange("p a b -> p (a b)"),
              W_all.rearrange("p a b -> p (a b)"), A_all[:, :]):
        st_tok = G(lambda E, a=a: E.memset(a, 0.0))
    U_ready = [st_tok] * H
    W_ready = [st_tok] * H
    A_ready = [st_tok] * H

    uses = {}

    def use(key, tok):
        uses.setdefault(key, []).append(tok)

    def take(key):
        return uses.pop(key, [])

    def head_slices(h, cp=0):
        qb = 64 * (h % 2)
        return dict(
            Ec=Ec2[cp][:, M * h:M * h + M],
            Vc=Vc2[cp][:, DV * h:DV * h + DV],
            Ktm=Ktm2[cp][:, DK * h:DK * h + DK],
            SG=SGc2[cp][:, DV * h:DV * h + DV],
            Qfm=Qfm[qb:qb + 64, h // 2, :],
            Kfm=Kfm[qb:qb + 64, h // 2, :],
            U=U_all[qb:qb + 64, h // 2, :],
            W=W_all[0:64, h, :],
            A=A_all[0:1, M * h:M * h + M],
            qb=qb,
        )

    def ln_transpose(x_sb, x_tok, dst_hi, dst_lo, key):
        wlist = [x_tok, const_tok] + take(key + "_tm")
        for i in range(nst):
            seg = min(512, D)
            V(lambda E, i=i, seg=seg: E.bn_stats(bnst[:, 6 * i:6 * i + 6],
                                                 x_sb[:, seg * i:seg * (i + 1)]),
              waits=wlist if i == 0 else [])
        V(lambda E: E.bn_aggr(mv2, bnst[:, 0:6 * nst]))
        V(lambda E: E.tensor_scalar(rs1, mv2[:, 1:2], EPS, None, ALU.add))
        V(lambda E: E.reciprocal(rs2, rs1))
        tok_sq = S(lambda E: E.activation(ln_a, rs2, AF.Sqrt),
                   waits=[now("ve", "vector")])
        V(lambda E: E.scalar_tensor_tensor(ln_b, mv2[:, 0:1], -1.0, ln_a,
                                           ALU.mult, ALU.mult),
          waits=[tok_sq])
        last = None
        half = D // NTH
        hc = half // 128
        for th in range(NTH):
            tok_n = S(lambda E, th=th: E.activation(
                xn_tm[:, 0:half], x_sb[:, half * th:half * (th + 1)],
                AF.Identity, bias=ln_b, scale=ln_a),
                waits=[x_tok, now("ve", "vector")] + take(key + "_tm"))
            for g0 in range(0, hc, 4):
                gn = min(4, hc - g0)
                bi, bank, bfree = pool.acq()
                tks = []
                for k in range(gn):
                    tks.append(T(lambda E, k=k, g0=g0, bank=bank:
                                 E.transpose(
                                     bank[:, 128 * k:128 * (k + 1)],
                                     xn_tm[:, 128 * (g0 + k):128 * (g0 + k + 1)],
                                     ident),
                                 waits=[tok_n, bfree, const_tok]))
                gg = th * hc + g0
                hv = S(lambda E, bank=bank, gg=gg, gn=gn:
                       E.copy(dst_hi[:, gg:gg + gn, :],
                              bank[:, 0:128 * gn].rearrange(
                                  "p (a b) -> p a b", a=gn)),
                       waits=[tks[-1]])
                lv = V(lambda E, bank=bank, gg=gg, gn=gn:
                       E.tensor_tensor(dst_lo[:, gg:gg + gn, :],
                                       bank[:, 0:128 * gn].rearrange(
                                           "p (a b) -> p a b", a=gn),
                                       dst_hi[:, gg:gg + gn, :],
                                       ALU.subtract),
                       waits=[tks[-1], hv])
                pool.rel(bi, lv)
                use(key + "_tm", lv)
                last = [hv, lv]
        return last

    def project_tm_sched(srch, srcl, src_tok, specs):
        work = []
        for name, whbm, ncols, evac, prev_uses in specs:
            for cg in range(0, ncols, 512):
                w = min(512, ncols - cg)
                work.append((name, whbm, cg, w, evac, prev_uses))
        return project_tm_sched_groups(srch, srcl, src_tok, work)

    def project_tm_sched_groups(srch, srcl, src_tok, work):
        nsl = DC // SLD
        flat = [(gi, sl) for gi in range(len(work)) for sl in range(nsl)]
        pend = {}

        def preload(ix):
            if ix >= len(flat) or ix in pend:
                return
            gi, sl = flat[ix]
            name, whbm, cg, w, _, _ = work[gi]
            hap = whbm.rearrange("(dc p) c -> p dc c", p=128)[
                :, sl * SLD:(sl + 1) * SLD, cg:cg + w]
            pend[ix] = s_w.load((name, cg, sl), hap, SLD * w)

        def group_thunk(gi):
            name, whbm, cg, w, evac, prev_uses = work[gi]
            if gi == 0:
                preload(0)
                preload(1)
                preload(2)
            bi, bank, bfree = pool.acq()
            first = True
            mt = None
            for sl in range(nsl):
                ix = gi * nsl + sl
                di, hi, lo, stoks = pend.pop(ix)
                preload(ix + 3)
                h4 = hi.rearrange("p (a b) -> p a b", a=SLD)
                l4 = lo.rearrange("p (a b) -> p a b", a=SLD)
                for dd in range(SLD):
                    d = sl * SLD + dd
                    w0 = (stoks + [bfree] + list(src_tok)) if first \
                        else (stoks if dd == 0 else [])
                    mt = mm3(bank[:, 0:w], srch[:, d, :], srcl[:, d, :],
                             h4[:, dd, :], l4[:, dd, :],
                             start=first, stop=(d == DC - 1), waits=w0)
                    first = False
                s_w.rel(di, mt)
            ev = evac(cg, w, bank, mt, prev_uses)
            pool.rel(bi, ev)
            return ev

        return [lambda gi=gi: group_thunk(gi) for gi in range(len(work))]

    def project_tm(srch, srcl, src_tok, specs):
        for t_ in project_tm_sched(srch, srcl, src_tok, specs):
            t_()

    for pg in range(NPG):
        xn_toks = []
        for pc in range(PPS):
            gidx = pg * PPS + pc
            xi, xslot, xtok = s_x.load(xp_t[gidx])
            tk = ln_transpose(xslot, xtok,
                              xnp_hi[:, :, 128 * pc:128 * (pc + 1)],
                              xnp_lo[:, :, 128 * pc:128 * (pc + 1)], "pre")
            s_x.rel(xi, tk)
            xn_toks.append(tk)
        xn_tok = xn_toks[-1]

        kvs_done = {}
        for pc in range(PPS):
            def pv_ev(cg, w, bank, mt, pu, pc=pc):
                ev = V(lambda E, b=bank, cg=cg, w=w, pc=pc:
                       E.tensor_copy(Vp[:, pc, cg:cg + w], b[:, 0:w]),
                       waits=[mt] + pu)
                kvs_done[("v", pc, cg // 512)] = ev
                return ev

            def pk_ev(cg, w, bank, mt, pu, pc=pc):
                ev = V(lambda E, b=bank, cg=cg, w=w, pc=pc:
                       E.tensor_copy(Kp[:, pc, cg:cg + w], b[:, 0:w]),
                       waits=[mt] + pu)
                kvs_done[("k", pc, cg // 512)] = ev
                return ev

            def ps_ev(cg, w, bank, mt, pu, pc=pc):
                cl = V(lambda E, b=bank, w=w: E.tensor_scalar(
                    b[:, 0:w], b[:, 0:w], 32.0, -32.0, ALU.min, ALU.max),
                    waits=[mt] + pu)
                ev = S(lambda E, b=bank, cg=cg, w=w, pc=pc:
                       E.activation(Ep[:, pc, cg:cg + w], b[:, 0:w],
                                    AF.Exp, bias=-32.0), waits=[cl])
                kvs_done[("s", pc, cg // 512)] = ev
                return ev

            project_tm(xnp_hi[:, :, 128 * pc:128 * (pc + 1)],
                       xnp_lo[:, :, 128 * pc:128 * (pc + 1)], xn_tok, [
                           ("wk", wk, H * DK, pk_ev, []),
                           ("wv", wv, H * DV, pv_ev, []),
                           ("ws", ws, H * M, ps_ev, []),
                       ])

        for pc in range(PPS):
            for h in range(H):
                hs = head_slices(h)
                qb = hs["qb"]
                Ep_h = Ep[:, pc, M * h:M * h + M]
                Kp_h = Kp[:, pc, DK * h:DK * h + DK]
                Vp_h = Vp[:, pc, DV * h:DV * h + DV]
                ew = [kvs_done[("s", pc, (M * h) // 512)],
                      kvs_done[("k", pc, (DK * h) // 512)],
                      kvs_done[("v", pc, (DV * h) // 512)]]
                bi, bank, bfree = pool.acq()
                mt = T(lambda E, b=bank, l=Kp_h, r=Ep_h, qb=qb:
                       E.matmul(b[qb:qb + 64, 0:M], l, r, start=True,
                                stop=True),
                       waits=ew + [bfree, U_ready[h]])
                ev = V(lambda E, b=bank, U=hs["U"], qb=qb:
                       E.tensor_tensor(U, U, b[qb:qb + 64, 0:M], ALU.add),
                       waits=[mt, U_ready[h]])
                U_ready[h] = ev
                pool.rel(bi, ev)
                bi, bank, bfree = pool.acq()
                mt = T(lambda E, b=bank, l=Ep_h, r=Vp_h:
                       E.matmul(b[0:64, 0:DV], l, r, start=True, stop=True),
                       waits=ew + [bfree, W_ready[h]])
                ev = V(lambda E, b=bank, W=hs["W"]:
                       E.tensor_tensor(W, W, b[0:64, 0:DV], ALU.add),
                       waits=[mt, W_ready[h]])
                W_ready[h] = ev
                pool.rel(bi, ev)
                bi, bank, bfree = pool.acq()
                T(lambda E, b=bank, r=Ep_h:
                  E.matmul(b[0:1, 0:M], onesb[:, 0:1], r, start=True,
                           stop=False),
                  waits=ew + [bfree, A_ready[h], const_tok])
                mt = T(lambda E, b=bank, A=hs["A"]:
                       E.matmul(b[0:1, 0:M], onesb[0:1, 0:1], A,
                                start=False, stop=True))
                ev = V(lambda E, b=bank, A=hs["A"]:
                       E.tensor_copy(A, b[0:1, 0:M]), waits=[mt, A_ready[h]])
                A_ready[h] = ev
                pool.rel(bi, ev)
        P.barrier()

    fU = V(lambda E: E.tensor_scalar(
        U_all.rearrange("p a b -> p (a b)"),
        U_all.rearrange("p a b -> p (a b)"),
        flag[:, 0:1], None, ALU.mult),
        waits=[U_ready[H - 1], const_tok])
    fW = V(lambda E: E.tensor_scalar(
        W_all.rearrange("p a b -> p (a b)"),
        W_all.rearrange("p a b -> p (a b)"),
        flag[0:64, 0:1], None, ALU.mult), waits=[W_ready[H - 1]])
    fA = V(lambda E: E.tensor_scalar(A_all, A_all, flag[0:1, 0:1], None,
                                     ALU.mult),
           waits=[A_ready[H - 1]])
    U_ready = [fU] * H
    W_ready = [fW] * H
    A_ready = [fA] * H

    res_free = [[] for _ in range(CPS)]
    for sg in range(NSG):
        P.barrier()
        xn_toks = []
        for tt in range(CPS):
            gidx = sg * CPS + tt
            xi, xslot, xtok = s_x.load(xo_t[gidx])
            tk = ln_transpose(xslot, xtok,
                              xn_hi[:, :, 128 * tt:128 * (tt + 1)],
                              xn_lo[:, :, 128 * tt:128 * (tt + 1)], "ffn")
            s_x.rel(xi, tk)
            xn_toks.append(tk)
        xn_tok = xn_toks[-1]

        comb_tok = [None] * CPS
        shg_tok = [None] * CPS
        prev_router_v = None
        for tt in range(CPS):
            bi, bank, bfree = pool.acq()
            mt = None
            for d in range(DC):
                mt = mm3(bank[:, 0:NEXP + 1],
                         xn_hi[:, d, 128 * tt:128 * (tt + 1)],
                         xn_lo[:, d, 128 * tt:128 * (tt + 1)],
                         wrt_hi[:, d, :], wrt_lo[:, d, :],
                         start=(d == 0), stop=(d == DC - 1),
                         waits=(list(xn_toks[tt]) + [wrt_tok, bfree])
                         if d == 0 else [])
            V(lambda E, b=bank: E.reduce_max(sm_m, b[:, 0:NEXP], AX.X,
                                             negate=True), waits=[mt])
            t_pe = S(lambda E, b=bank: E.activation(per, b[:, 0:NEXP], AF.Exp,
                                                    bias=sm_m,
                                                    accum_out=sm_s),
                     waits=[mt, now("ve", "vector"), prev_router_v])
            t_sg = S(lambda E, b=bank, tt=tt:
                     E.activation(shg[:, tt:tt + 1], b[:, NEXP:NEXP + 1],
                                  AF.Sigmoid), waits=[mt])
            shg_tok[tt] = t_sg
            V(lambda E: E.reciprocal(sm_r, sm_s), waits=[t_pe])
            t_pr = V(lambda E: E.tensor_scalar(probs, per, sm_r, None,
                                               ALU.mult))
            pool.rel(bi, [t_pr, t_sg])
            V(lambda E: E.tensor_copy(cur, probs))
            for j in range(TOPK):
                V(lambda E: E.reduce_max(sm_m, cur, AX.X))
                V(lambda E: E.tensor_scalar(eqb, cur, sm_m, None,
                                            ALU.is_equal))
                if j == 0:
                    V(lambda E: E.tensor_copy(sel, eqb))
                else:
                    V(lambda E: E.tensor_tensor(sel, sel, eqb, ALU.add))
                if j < TOPK - 1:
                    V(lambda E: E.scalar_tensor_tensor(cur, eqb, -1e30, cur,
                                                       ALU.mult, ALU.add))
            V(lambda E: E.tensor_tensor(w4, probs, sel, ALU.mult))
            V(lambda E: E.reduce_sum(sm_s, w4, AX.X))
            V(lambda E: E.reciprocal(sm_r, sm_s))
            comb_tok[tt] = V(lambda E, tt=tt: E.tensor_scalar(
                comb[:, tt, :], w4, sm_r, None, ALU.mult))
            prev_router_v = comb_tok[tt]

        res_written = [[False] * NDS for _ in range(CPS)]

        def gu_phase(wg_h, wu_h, nj, col0, kg, ku):
            a_toks = []
            prev_hi = None
            prev_lo = None
            pend = {}

            def preload(j):
                if j >= nj:
                    return
                pend[j] = (s_w.load((kg, col0 + 128 * j),
                                    colslice(wg_h, col0 + 128 * j), DC * 128),
                           s_w.load((ku, col0 + 128 * j),
                                    colslice(wu_h, col0 + 128 * j), DC * 128))

            preload(0)
            preload(1)
            for j in range(nj):
                (dg, ghi, glo, tg), (du, uhi, ulo, tu) = pend.pop(j)
                preload(j + 2)
                gh = ghi.rearrange("p (a b) -> p a b", a=DC)
                gl = glo.rearrange("p (a b) -> p a b", a=DC)
                uh = uhi.rearrange("p (a b) -> p a b", a=DC)
                ul = ulo.rearrange("p (a b) -> p a b", a=DC)
                bg, bankg, bfg = pool.acq()
                mg = None
                for d in range(DC):
                    mg = mm3(bankg[:, 0:SGT], gh[:, d, :], gl[:, d, :],
                             xn_hi[:, d, :], xn_lo[:, d, :],
                             start=(d == 0), stop=(d == DC - 1),
                             waits=(tg + list(xn_tok) + [bfg])
                             if d == 0 else [])
                s_w.rel(dg, mg)
                bu, banku, bfu = pool.acq()
                mu = None
                for d in range(DC):
                    mu = mm3(banku[:, 0:SGT], uh[:, d, :], ul[:, d, :],
                             xn_hi[:, d, :], xn_lo[:, d, :],
                             start=(d == 0), stop=(d == DC - 1),
                             waits=(tu + [bfu]) if d == 0 else [])
                s_w.rel(du, mu)
                t_sil = S(lambda E, b=bankg: E.activation(silu_t, b[:, 0:SGT],
                                                          AF.Silu),
                          waits=[mg, prev_hi, prev_lo])
                pool.rel(bg, t_sil)
                tA = V(lambda E, b=banku: E.tensor_tensor(
                    silu_t, silu_t, b[:, 0:SGT], ALU.mult),
                    waits=[mu, t_sil, prev_lo])
                hv = S(lambda E, j=j: E.copy(act_hi[:, j, :], silu_t),
                       waits=[tA] + take(f"act{j}"))
                lv = V(lambda E, j=j: E.tensor_tensor(
                    act_lo[:, j, :], silu_t, act_hi[:, j, :], ALU.subtract),
                    waits=[hv])
                pool.rel(bu, lv)
                prev_hi = hv
                prev_lo = lv
                a_toks.append([hv, lv])
            return a_toks

        def down_phase(get_slice, nj, scal_for_tt, sc_tok, act_tok,
                       first_write, kdown):
            JG = max(1, min(nj, SLOT // DSEG))
            pend = {}

            def preload(ds):
                if ds >= NDS:
                    return
                sl = []
                for j0 in range(0, nj, JG):
                    jn = min(JG, nj - j0)
                    di, whi, wlo, stoks = s_w.load((kdown, ds, j0),
                                                   get_slice(ds, j0, jn),
                                                   jn * DSEG)
                    sl.append((di,
                               whi.rearrange("p (a b) -> p a b", a=jn),
                               wlo.rearrange("p (a b) -> p a b", a=jn),
                               stoks, j0, jn))
                pend[ds] = sl

            preload(0)
            for ds in range(NDS):
                slots = pend.pop(ds)
                preload(ds + 1)
                last_mt = None
                for tt in range(CPS):
                    bi, bank, bfree = pool.acq()
                    mt = None
                    first = True
                    for (di, wh, wl, stoks, j0, jn) in slots:
                        for jj in range(jn):
                            j = j0 + jj
                            w0 = []
                            if first:
                                w0 = (stoks + [bfree] + act_tok
                                      + res_free[tt])
                            elif jj == 0:
                                w0 = list(stoks)
                            mt = mm3(bank[:, 0:DSEG],
                                     act_hi[:, j, 128 * tt:128 * (tt + 1)],
                                     act_lo[:, j, 128 * tt:128 * (tt + 1)],
                                     wh[:, jj, :], wl[:, jj, :],
                                     start=first, stop=(j == nj - 1),
                                     waits=w0)
                            first = False
                    last_mt = mt
                    sc = scal_for_tt(tt)
                    dst = res[:, tt, ds * DSEG:(ds + 1) * DSEG]
                    if first_write and not res_written[tt][ds]:
                        ev = V(lambda E, b=bank, dst=dst, sc=sc:
                               E.tensor_scalar(dst, b[:, 0:DSEG], sc, None,
                                               ALU.mult),
                               waits=[mt, sc_tok] + res_free[tt])
                        res_written[tt][ds] = True
                    else:
                        ev = V(lambda E, b=bank, dst=dst, sc=sc:
                               E.scalar_tensor_tensor(dst, b[:, 0:DSEG], sc,
                                                      dst, ALU.mult, ALU.add),
                               waits=[mt, sc_tok])
                    pool.rel(bi, ev)
                    use(f"res{tt}", ev)
                for (di, _, _, _, _, _) in slots:
                    s_w.rel(di, last_mt)

        for e in range(NEXP):
            a_toks = gu_phase(wgate[e], wup[e], FJ, 0, ("g", e), ("u", e))
            down_phase(lambda ds, j0, jn, e=e: wd_slice(wdown[e], ds, j0, jn),
                       FJ, lambda tt, e=e: comb[:, tt, e:e + 1],
                       comb_tok[CPS - 1], a_toks[-1], first_write=(e == 0),
                       kdown=("d", e))
            dn = now("mm", "tensor")
            for j in range(FJ):
                use(f"act{j}", dn)

        j0 = 0
        while j0 < SHJ:
            ng = min(SHGRP, SHJ - j0)
            a_toks = gu_phase(wsg, wsu, ng, 128 * j0, "sg", "su")
            down_phase(lambda ds, jj0, jn, j0=j0:
                       wd_slice(wsd, ds, j0 + jj0, jn), ng,
                       lambda tt: shg[:, tt:tt + 1],
                       shg_tok[CPS - 1], a_toks[-1], first_write=False,
                       kdown=("sd", j0))
            dn = now("mm", "tensor")
            for j in range(ng):
                use(f"act{j}", dn)
            j0 += ng

        P.barrier()
        for ci in range(CPS):
            gc = sg * CPS + ci
            cp = 0
            xnch, xncl = xnc_hi2[cp], xnc_lo2[cp]
            VcP, SGcP, EcP, KtmP = Vc2[cp], SGc2[cp], Ec2[cp], Ktm2[cp]
            xi, x_sb, xtok = s_x.load(xo_t[gc])
            tk_xn = ln_transpose(x_sb, xtok, xnch, xncl, "att")

            pV = take(f"Vc{cp}"); pE = take(f"Ec{cp}"); pG = take(f"SGc{cp}")
            pQ = take("Qtm"); pK = take("Ktm0")

            def ev_v(cg, w, bank, mt, pu, dst=VcP):
                return V(lambda E, b=bank, cg=cg, w=w, dst=dst:
                         E.tensor_copy(dst[:, cg:cg + w], b[:, 0:w]),
                         waits=[mt] + pu)

            def ev_s(cg, w, bank, mt, pu, dst=EcP):
                cl = V(lambda E, b=bank, w=w: E.tensor_scalar(
                    b[:, 0:w], b[:, 0:w], 32.0, -32.0, ALU.min, ALU.max),
                    waits=[mt] + pu)
                return S(lambda E, b=bank, cg=cg, w=w, dst=dst:
                         E.activation(dst[:, cg:cg + w], b[:, 0:w],
                                      AF.Exp, bias=-32.0), waits=[cl])

            def ev_g(cg, w, bank, mt, pu, dst=SGcP):
                return S(lambda E, b=bank, cg=cg, w=w, dst=dst:
                         E.activation(dst[:, cg:cg + w], b[:, 0:w],
                                      AF.Silu), waits=[mt] + pu)

            def ev_q(cg, w, bank, mt, pu):
                return S(lambda E, b=bank, cg=cg, w=w:
                         E.mul(Q_tm[:, cg:cg + w], b[:, 0:w], 0.125),
                         waits=[mt] + pu)

            def ev_k(cg, w, bank, mt, pu, dst=KtmP):
                return V(lambda E, b=bank, cg=cg, w=w, dst=dst:
                         E.tensor_copy(dst[:, cg:cg + w], b[:, 0:w]),
                         waits=[mt] + pu)

            project_tm(xnch, xncl, tk_xn, [
                ("wq", wq, H * DK, ev_q, pQ),
                ("wk", wk, H * DK, ev_k, pK),
                ("ws", ws, H * M, ev_s, pE),
            ])

            for cc in range(KC):
                bi, bank, bfree = pool.acq()
                tk = T(lambda E, b=bank, cc=cc:
                       E.transpose(b[:, 0:128],
                                   Q_tm[:, 128 * cc:128 * (cc + 1)], ident),
                       waits=[now("ve", "vector"), now("ac", "scalar"),
                              bfree])
                ev = V(lambda E, b=bank, cc=cc:
                       E.tensor_copy(Qfm[:, cc, :], b[:, 0:128]),
                       waits=[tk] + take(f"Qfm{cc}"))
                pool.rel(bi, ev)
                use("Qtm", tk)
                bi, bank, bfree = pool.acq()
                tk = T(lambda E, b=bank, cc=cc, src_=KtmP:
                       E.transpose(b[:, 0:128],
                                   src_[:, 128 * cc:128 * (cc + 1)], ident),
                       waits=[bfree])
                ev = V(lambda E, b=bank, cc=cc:
                       E.tensor_copy(Kfm[:, cc, :], b[:, 0:128]),
                       waits=[tk] + take(f"Kfm{cc}"))
                pool.rel(bi, ev)
                use("Ktm0", tk)

            proj_done = [now("ve", "vector"), now("ac", "scalar")]

            vc_toks = {}
            sg_toks = {}

            def ev_v2(cg, w, bank, mt, pu):
                t = ev_v(cg, w, bank, mt, pu)
                vc_toks[cg // 512] = t
                return t

            def ev_g2(cg, w, bank, mt, pu):
                t = ev_g(cg, w, bank, mt, pu)
                sg_toks[cg // 512] = t
                return t

            vg_work = []
            nvg = (H * DV + 511) // 512
            for gidx2 in range(nvg):
                cg0 = 512 * gidx2
                w0_ = min(512, H * DV - cg0)
                vg_work.append(("wv", wv, cg0, w0_, ev_v2, pV if gidx2 == 0 else []))
                vg_work.append(("wg", wg, cg0, w0_, ev_g2, pG if gidx2 == 0 else []))
            vg_thunks = project_tm_sched_groups(xnch, xncl, tk_xn, vg_work)

            hst = {}

            def stA(h):
                p = h % 2
                hs = head_slices(h, cp)
                d = {"hs": hs}
                A_h = hs["A"]; Ec_h = hs["Ec"]
                biA, bA, bfA = pool.acq()
                T(lambda E, b=bA, A=A_h: E.matmul(b[:, 0:M],
                                                  onesb[0:1, 0:128],
                                                  A, start=True, stop=False),
                  waits=[A_ready[h], bfA, const_tok] + proj_done)
                T(lambda E, b=bA, Eh=Ec_h: E.matmul(
                    b[:, 0:M], utri, Eh, start=False, stop=True))
                T(lambda E, b=bA, Eh=Ec_h: E.matmul(
                    b[0:1, M:2 * M], onesb[:, 0:1], Eh, start=True,
                    stop=False, skip_group_check=True))
                tokN = T(lambda E, b=bA, A=A_h: E.matmul(
                    b[0:1, M:2 * M], onesb[0:1, 0:1], A, start=False,
                    stop=True, skip_group_check=True))
                biM, bM, bfM = pool.acq()
                tokAm = T(lambda E, b=bM, K=hs["Kfm"], Q=hs["Qfm"]:
                          E.matmul(b[:, 0:128], K, Q, start=True, stop=True),
                          waits=[bfM])
                d["tokR"] = V(lambda E, b=bA, p=p: E.reciprocal(
                    R2[p][:, 0:M], b[:, 0:M]), waits=[tokN])
                tokAh = V(lambda E, b=bA, A=A_h: E.tensor_copy(
                    A, b[0:1, M:2 * M]), waits=[tokN, A_ready[h]])
                A_ready[h] = tokAh
                pool.rel(biA, [d["tokR"], tokAh])
                d["tokAmm"] = V(lambda E, b=bM, p=p: E.tensor_tensor(
                    AmaskT2[p], b[:, 0:128], utri, ALU.mult), waits=[tokAm])
                pool.rel(biM, d["tokAmm"])
                hst[h] = d

            def stB(h):
                p = h % 2
                d = hst[h]; hs = d["hs"]
                Ec_h = hs["Ec"]
                biG, bG, bfG = pool.acq()
                T(lambda E, b=bG, Q=hs["Qfm"], U=hs["U"]:
                  E.matmul(b[:, 0:M], Q, U, start=True, stop=False),
                  waits=[U_ready[h], bfG])
                T(lambda E, b=bG, Eh=Ec_h, p=p:
                  E.matmul(b[:, 0:M], AmaskT2[p], Eh, start=False,
                           stop=True),
                  waits=[d["tokAmm"]])
                bU = bG
                tokUm = T(lambda E, b=bU, K=hs["Ktm"], Eh=Ec_h, qb=hs["qb"]:
                          E.matmul(b[qb:qb + 64, M:M + M], K, Eh, start=True,
                                   stop=True, skip_group_check=True))
                tokok = V(lambda E, b=bG, p=p: E.tensor_tensor(
                    ok2[p][:, 0:M], b[:, 0:M], R2[p][:, 0:M], ALU.mult),
                    waits=[tokUm, d["tokR"]])
                V(lambda E, p=p: E.reduce_max(cm_m[p], ok2[p][:, 0:M], AX.X,
                                              negate=True))
                tokpe = S(lambda E, p=p: E.activation(
                    pe2[p][:, 0:M], ok2[p][:, 0:M], AF.Exp, bias=cm_m[p],
                    accum_out=cm_s[p]), waits=[now("ve", "vector")])
                V(lambda E, p=p: E.reciprocal(cm_r[p], cm_s[p]),
                  waits=[tokpe])
                d["tokPR"] = V(lambda E, p=p: E.scalar_tensor_tensor(
                    PR2[p][:, 0:M], pe2[p][:, 0:M], cm_r[p], R2[p][:, 0:M],
                    ALU.mult, ALU.mult))
                ev = V(lambda E, b=bU, U=hs["U"], qb=hs["qb"]:
                       E.tensor_tensor(U, U, b[qb:qb + 64, M:M + M], ALU.add),
                       waits=[tokUm])
                U_ready[h] = ev
                pool.rel(biG, [tokok, ev])

            def stC(h):
                p = h % 2
                d = hst[h]; hs = d["hs"]
                biP, bP, bfP = pool.acq()
                tkP = T(lambda E, b=bP, p=p: E.transpose(
                    b[0:M, 0:128], PR2[p][:, 0:M], ident),
                    waits=[d["tokPR"], bfP])
                evP = V(lambda E, b=bP, p=p: E.tensor_copy(
                    PRT2[p][0:M, :], b[0:M, 0:128]), waits=[tkP])
                pool.rel(biP, evP)
                biE, bE, bfE = pool.acq()
                tkE = T(lambda E, b=bE, Eh=hs["Ec"]: E.transpose(
                    b[0:M, 0:128], Eh, ident), waits=[bfE])
                evE = V(lambda E, b=bE, p=p: E.tensor_copy(
                    EcT2[p][0:M, :], b[0:M, 0:128]), waits=[tkE])
                pool.rel(biE, evE)
                biD, bD, bfD = pool.acq()
                tokD = T(lambda E, b=bD, p=p: E.matmul(
                    b[:, 0:128], EcT2[p][0:M, :], PRT2[p][0:M, :],
                    start=True, stop=True),
                    waits=[evP, evE, bfD])
                d["tokDm"] = V(lambda E, b=bD, p=p: E.tensor_tensor(
                    DT2[p], b[:, 0:128], utri, ALU.mult), waits=[tokD])
                pool.rel(biD, d["tokDm"])

            def stD(h):
                p = h % 2
                d = hst[h]; hs = d["hs"]
                biO, bO, bfO = pool.acq()
                gi4 = (DV * h) // 512
                T(lambda E, b=bO, Vh=hs["Vc"], p=p: E.matmul(
                    b[:, 0:DV], DT2[p], Vh, start=True, stop=False),
                    waits=[d["tokDm"], bfO, vc_toks[gi4], sg_toks[gi4]])
                tokO = T(lambda E, b=bO, W=hs["W"], p=p: E.matmul(
                    b[:, 0:DV], PRT2[p][0:M, :], W, start=False, stop=True),
                    waits=[W_ready[h]])
                bW = bO
                tokWm = T(lambda E, b=bW, Eh=hs["Ec"], Vh=hs["Vc"]:
                          E.matmul(b[0:64, DV:2 * DV], Eh, Vh, start=True,
                                   stop=True, skip_group_check=True))
                tokSq = S(lambda E, b=bO, p=p: E.activation(
                    sqd[p], b[:, 0:DV], AF.Square, accum_out=cr1[p]),
                    waits=[tokWm])
                V(lambda E, p=p: E.tensor_scalar(cr2[p], cr1[p], 1.0 / DV,
                                                 EPS, ALU.mult, ALU.add),
                  waits=[tokSq])
                V(lambda E, p=p: E.reciprocal(cr1[p], cr2[p]))
                tokRs = S(lambda E, p=p: E.activation(crs[p], cr1[p],
                                                      AF.Sqrt),
                          waits=[now("ve", "vector")])
                toko1 = V(lambda E, b=bO, p=p: E.scalar_tensor_tensor(
                    o12[p][:, 0:DV], b[:, 0:DV], crs[p], gnwr, ALU.mult,
                    ALU.mult),
                    waits=[tokRs, const_tok])
                d["tokof"] = V(lambda E, p=p, SG=hs["SG"]: E.tensor_tensor(
                    of2[p][:, 0:DV], o12[p][:, 0:DV], SG, ALU.mult))
                evW = V(lambda E, b=bW, W=hs["W"]:
                        E.tensor_tensor(W, W, b[0:64, DV:2 * DV], ALU.add),
                        waits=[tokWm])
                W_ready[h] = evW
                pool.rel(biO, [toko1, evW])

            def stE(h):
                p = h % 2
                d = hst[h]
                biT, bT, bfT = pool.acq()
                tkT = T(lambda E, b=bT, p=p: E.transpose(
                    b[:, 0:DV], of2[p][:, 0:DV], ident),
                    waits=[d["tokof"], bfT])
                hvT = S(lambda E, b=bT, h=h: E.copy(ofm_hi[:, h, :],
                                                    b[:, 0:DV]),
                        waits=[tkT] + take(f"ofm{h}"))
                lvT = V(lambda E, b=bT, h=h: E.tensor_tensor(
                    ofm_lo[:, h, :], b[:, 0:DV], ofm_hi[:, h, :],
                    ALU.subtract),
                    waits=[hvT])
                pool.rel(biT, lvT)
                di, whi, wlo, stoks = s_w.load(("wo", h), wo_t[h], D)
                for dsg in range(nseg):
                    bi, bank, bfree = wo_banks[dsg]
                    w0 = stoks + [hvT, lvT] + (res_free[ci] + bfree
                                               if h == 0 else [])
                    wo_mts[dsg] = mm3(
                        bank[:, 0:segw],
                        ofm_hi[:, h, :], ofm_lo[:, h, :],
                        whi[:, segw * dsg:segw * (dsg + 1)],
                        wlo[:, segw * dsg:segw * (dsg + 1)],
                        start=(h == 0), stop=(h == H - 1), waits=w0)
                s_w.rel(di, wo_mts[-1])
                use(f"ofm{h}", wo_mts[-1])

            nseg = max(D // 512, 1)
            segw = min(512, D)
            wo_banks = [pool.acq() for _ in range(nseg)]
            wo_mts = [None] * nseg

            vg_i = 0
            for h0 in range(0, H, 2):
                need = ((min(h0 + 1, H - 1)) * DV) // 512 + 1
                while vg_i < 2 * need and vg_i < len(vg_thunks):
                    vg_thunks[vg_i]()
                    vg_i += 1
                for st in (stA, stB, stC, stD, stE):
                    st(h0)
                    if h0 + 1 < H:
                        st(h0 + 1)
            while vg_i < len(vg_thunks):
                vg_thunks[vg_i]()
                vg_i += 1

            core_mm = now("mm", "tensor")
            core_ve = now("ve", "vector")
            use(f"Vc{cp}", core_mm)
            use(f"SGc{cp}", core_ve)
            use(f"Ec{cp}", core_mm)
            use("Ktm0", core_mm)
            for cc in range(KC):
                use(f"Qfm{cc}", core_mm)
                use(f"Kfm{cc}", core_mm)

            last_mts = wo_mts
            fl_toks = []
            for dsg in range(nseg):
                bi, bank, bfree = wo_banks[dsg]
                t1 = V(lambda E, b=bank, dsg=dsg, x=x_sb, segw=segw:
                       E.tensor_tensor(af32[:, 0:segw], b[:, 0:segw],
                                       x[:, segw * dsg:segw * (dsg + 1)],
                                       ALU.add),
                       waits=[last_mts[dsg], xtok] + take(f"res{ci}")
                       + res_free[ci])
                t2 = V(lambda E, dsg=dsg, ci=ci, segw=segw:
                       E.tensor_tensor(res[:, ci, segw * dsg:segw * (dsg + 1)],
                                       res[:, ci, segw * dsg:segw * (dsg + 1)],
                                       af32[:, 0:segw], ALU.add))
                pool.rel(bi, t2)
                fl_toks.append(t2)
            s_x.rel(xi, fl_toks[-1])
            otok = P.op("gpsimd",
                        lambda E, ci=ci, gc=gc: E.dma_start(out=out_t[gc],
                                                            in_=res[:, ci, :]),
                        waits=fl_toks, inc=f"so{ci}", amt=16)
            res_free[ci] = [otok]
        P.barrier()

    for ci in range(CPS):
        P.wait_only("gpsimd", (f"so{ci}", P.cnt[f"so{ci}"], None))

    import contextlib
    with nc.Block() as block:
        with contextlib.ExitStack() as stack:
            P.sems = {s: stack.enter_context(nc.semaphore(f"s_{s}"))
                      for s in SEMS}

            @block.tensor
            def _(eng):
                for f in P.stream["tensor"]:
                    f(eng)

            @block.vector
            def _(eng):
                for f in P.stream["vector"]:
                    f(eng)

            @block.scalar
            def _(eng):
                for f in P.stream["scalar"]:
                    f(eng)

            @block.gpsimd
            def _(eng):
                for f in P.stream["gpsimd"]:
                    f(eng)

            @block.sync
            def _(eng):
                for f in P.stream["sync"]:
                    f(eng)

    return nc


def make_inputs_per_core(cfg, inputs):
    NC = cfg["NC"]; TO = cfg["TO"]; TP = cfg["TP"]; DV = cfg["DV"]
    x = np.ascontiguousarray(np.asarray(inputs["L0_attn_hidden_states"],
                                        np.float32))
    B = x.shape[0]
    halves = max(NC // B, 1)

    def cc(a):
        return np.ascontiguousarray(np.asarray(a, np.float32))

    shared = dict(
        wq=cc(inputs["L0_attn_Wq"]), wk=cc(inputs["L0_attn_Wk"]),
        wv=cc(inputs["L0_attn_Wv"]), ws=cc(inputs["L0_attn_Ws"]),
        wg=cc(inputs["L0_attn_Wg"]), wo=cc(inputs["L0_attn_Wo"]),
        wrt=np.ascontiguousarray(np.concatenate(
            [np.asarray(inputs["L0_ffn_w_router"], np.float32),
             np.asarray(inputs["L0_ffn_w_sh_expert_gate"], np.float32)],
            axis=1)),
        wgate=cc(inputs["L0_ffn_w_gate"]), wup=cc(inputs["L0_ffn_w_up"]),
        wdown=cc(inputs["L0_ffn_w_down"]),
        wsg=cc(inputs["L0_ffn_w_sh_gate"]), wsu=cc(inputs["L0_ffn_w_sh_up"]),
        wsd=cc(inputs["L0_ffn_w_sh_down"]),
        identH=np.eye(128, dtype=np.float32),
        utriH=np.ascontiguousarray(np.triu(np.ones((128, 128), np.float32))),
        onesH=np.ones((128, 128), np.float32),
        gnwrH=np.ascontiguousarray(np.tile(
            np.asarray(inputs["L0_attn_g_norm_weight"], np.float32)[None, :DV],
            (128, 1))),
        cvalsH=np.ascontiguousarray(np.tile(
            np.array([[0.0, -32.0, 1.0, 0.125]], np.float32), (128, 1))),
    )
    in_maps = []
    for c in range(NC):
        b = c // halves
        half = c % halves
        t0 = half * TO
        xo = np.ascontiguousarray(x[b, t0:t0 + TO, :])
        if half > 0:
            xpre = np.ascontiguousarray(x[b, t0 - TP:t0, :])
            fl = 1.0
        else:
            xpre = xo
            fl = 0.0
        m = dict(shared)
        m["xo"] = xo
        m["xp"] = xpre
        m["flagH"] = np.full((128, 1), fl, np.float32)
        in_maps.append(m)
    return in_maps


def kernel(**inputs):
    cfg = CFG_FULL
    nc = build(cfg)
    in_maps = make_inputs_per_core(cfg, inputs)
    res = run_bass_kernel_spmd(nc, in_maps, core_ids=list(range(cfg["NC"])))
    x = np.asarray(inputs["L0_attn_hidden_states"])
    B, Tt, Dd = x.shape
    out = np.empty((B, Tt, Dd), np.float32)
    halves = max(cfg["NC"] // B, 1)
    for c in range(cfg["NC"]):
        b = c // halves
        t0 = (c % halves) * cfg["TO"]
        out[b, t0:t0 + cfg["TO"], :] = res.results[c]["out"]
    return out


# version 19
# speedup vs baseline: 1.1411x; 1.1411x over previous
import os

import numpy as np

import concourse.bass as bass
import concourse.mybir as mybir
from concourse.bass_utils import run_bass_kernel_spmd

F32 = mybir.dt.float32
BF16 = mybir.dt.bfloat16
AF = mybir.ActivationFunctionType
ALU = mybir.AluOpType
AX = mybir.AxisListType

ENGS = ("tensor", "vector", "scalar", "gpsimd", "sync")

CFG_FULL = dict(
    NC=8, D=2048, TO=1024, TP=1024, SGT=512, PGT=256,
    H=16, DK=64, DV=128, M=64,
    E=16, F=1024, TOPK=4, SHF=5632, DSEG=512, SHGRP=8,
)

CFG_SMALL = dict(
    NC=2, D=256, TO=256, TP=256, SGT=256, PGT=256,
    H=2, DK=64, DV=128, M=64,
    E=4, F=256, TOPK=2, SHF=384, DSEG=256, SHGRP=2,
)


class Prog:

    def __init__(self, nc):
        self.nc = nc
        self.stream = {e: [] for e in ENGS}
        self.cnt = {}
        self.hi = {e: {} for e in ENGS}
        self.sems = {}

    def sem(self, name):
        self.cnt[name] = 0

    def _wait(self, eng, tok):
        if tok is None:
            return
        s, v, _ = tok
        if self.hi[eng].get(s, 0) >= v:
            return
        self.hi[eng][s] = v

        def w(E, s=s, v=v):
            E.wait_ge(self.sems[s], v)

        self.stream[eng].append(w)

    def op(self, eng, fn, waits=(), inc=None, amt=1):
        for t in waits:
            if t is None:
                continue
            if isinstance(t, tuple):
                self._wait(eng, t)
            else:
                for tt in t:
                    self._wait(eng, tt)
        tok = None
        if inc is not None:
            self.cnt[inc] += amt
            tok = (inc, self.cnt[inc], eng)

        def run(E, fn=fn, inc=inc, amt=amt):
            r = fn(E)
            if inc is not None:
                r.then_inc(self.sems[inc], amt)

        self.stream[eng].append(run)
        return tok

    def wait_only(self, eng, tok):
        self._wait(eng, tok)

    def barrier(self):
        toks = [(s, self.cnt[s], None) for s in ("mm", "ve", "ac", "gp")]
        for e in ("tensor", "vector", "scalar", "gpsimd"):
            for t in toks:
                self._wait(e, t)


class Pool:

    def __init__(self, banks):
        self.banks = banks
        self.free = [[] for _ in banks]
        self.order = list(range(len(banks)))

    def acq(self):
        i = self.order.pop(0)
        return i, self.banks[i], list(self.free[i])

    def rel(self, i, tok):
        self.free[i] = tok if isinstance(tok, list) else [tok]
        self.order.append(i)


def build(cfg):
    D = cfg["D"]; DC = D // 128
    TO = cfg["TO"]; TP = cfg["TP"]
    SGT = cfg["SGT"]; PGT = cfg["PGT"]
    NSG = TO // SGT; NPG = TP // PGT
    CPS = SGT // 128
    PPS = PGT // 128
    H = cfg["H"]; DK = cfg["DK"]; DV = cfg["DV"]; M = cfg["M"]
    NEXP = cfg["E"]; F = cfg["F"]; FJ = F // 128
    TOPK = cfg["TOPK"]; SHF = cfg["SHF"]; SHJ = SHF // 128
    DSEG = cfg["DSEG"]; NDS = D // DSEG
    SHGRP = cfg["SHGRP"]
    KC = H * DK // 128
    VC = H * DV // 128
    SC = H * M // 128
    EPS = 1e-5
    SLD = min(4, DC)
    SLOT = max(D, 2 * DSEG, SLD * 512)
    assert DSEG <= SLOT and D % 128 == 0

    nc = bass.Bass("TRN2", target_bir_lowering=False, debug=False)

    def din(name, shape):
        return nc.dram_tensor(name, shape, F32, kind="ExternalInput").ap()

    xo = din("xo", [TO, D])
    xp = din("xp", [TP, D])
    wq = din("wq", [D, H * DK]); wk = din("wk", [D, H * DK])
    wv = din("wv", [D, H * DV]); ws = din("ws", [D, H * M])
    wg = din("wg", [D, H * DV]); wo = din("wo", [H * DV, D])
    wrt = din("wrt", [D, NEXP + 1])
    wgate = din("wgate", [NEXP, D, F]); wup = din("wup", [NEXP, D, F])
    wdown = din("wdown", [NEXP, F, D])
    wsg = din("wsg", [D, SHF]); wsu = din("wsu", [D, SHF])
    wsd = din("wsd", [SHF, D])
    identH = din("identH", [128, 128]); utriH = din("utriH", [128, 128])
    onesH = din("onesH", [128, 128]); gnwrH = din("gnwrH", [128, DV])
    flagH = din("flagH", [128, 1]); cvalsH = din("cvalsH", [128, 4])
    out = nc.dram_tensor("out", [TO, D], F32, kind="ExternalOutput").ap()

    xo_t = xo.rearrange("(n p) d -> n p d", p=128)
    xp_t = xp.rearrange("(n p) d -> n p d", p=128)
    out_t = out.rearrange("(n p) d -> n p d", p=128)
    wo_t = wo.rearrange("(v p) d -> v p d", p=128)

    def colslice(w, c0):
        return w.rearrange("(dc p) c -> p dc c", p=128)[:, :, c0:c0 + 128]

    def wd_slice(w3, ds, j0, jn):
        return w3.rearrange("(fj p) d -> p fj d", p=128)[:, j0:j0 + jn,
                                                         ds * DSEG:(ds + 1) * DSEG]

    def sb(name, shape, dt=F32):
        return nc.alloc_sbuf_tensor(name, shape, dt).ap()

    ident = sb("ident", [128, 128]); utri = sb("utri", [128, 128])
    onesb = sb("onesb", [128, 128]); gnwr = sb("gnwr", [128, DV])
    flag = sb("flag", [128, 1]); cvals = sb("cvals", [128, 4])
    nc.const_aps.register(F32, 0.0, cvals[:, 0:1])
    nc.const_aps.register(F32, -32.0, cvals[:, 1:2])
    nc.const_aps.register(F32, 1.0, cvals[:, 2:3])
    nc.const_aps.register(F32, 0.125, cvals[:, 3:4])
    wrt_hi = sb("wrt_hi", [128, DC, NEXP + 1], BF16)
    wrt_lo = sb("wrt_lo", [128, DC, NEXP + 1], BF16)

    nst = max(D // 512, 1)
    bnst = sb("bnst", [128, 6 * nst])
    mv2 = sb("mv2", [128, 2])
    ln_a = sb("ln_a", [128, 1]); ln_b = sb("ln_b", [128, 1])
    sm_m = sb("sm_m", [128, 1]); sm_s = sb("sm_s", [128, 1]); sm_r = sb("sm_r", [128, 1])
    rs1 = sb("rs1", [128, 1]); rs2 = sb("rs2", [128, 1]); rstd = sb("rstd", [128, 1])
    cm_m = [sb(f"cm_m{i}", [128, 1]) for i in range(2)]
    cm_s = [sb(f"cm_s{i}", [128, 1]) for i in range(2)]
    cm_r = [sb(f"cm_r{i}", [128, 1]) for i in range(2)]
    cr1 = [sb(f"cr1{i}", [128, 1]) for i in range(2)]
    cr2 = [sb(f"cr2{i}", [128, 1]) for i in range(2)]
    crs = [sb(f"crs{i}", [128, 1]) for i in range(2)]
    sqd = [sb(f"sqd{i}", [128, DV]) for i in range(2)]
    probs = sb("probs", [128, NEXP]); cur = sb("cur", [128, NEXP])
    sel = sb("sel", [128, NEXP]); eqb = sb("eqb", [128, NEXP]); w4 = sb("w4", [128, NEXP])
    per = sb("per", [128, NEXP])
    comb = sb("comb", [128, CPS, NEXP])
    shg = sb("shg", [128, CPS])
    silu_t = sb("silu_t", [128, SGT])
    af32 = silu_t

    U_all = sb("U_all", [128, max(H // 2, 1), DK])
    W_all = sb("W_all", [64, H, DV])
    A_all = sb("A_all", [1, H * M])

    res = sb("res", [128, CPS, D])
    xn_tm = sb("xn_tm", [128, min(D, 1024)])
    NTH = max(D // 1024, 1)

    NST = 3
    NSP = 6
    NWC = 4
    stg = [sb(f"stg{i}", [128, SLOT]) for i in range(NST)]
    spb = [sb(f"spb{i}", [128, 2 * SLOT], BF16) for i in range(NSP)]
    xr = [sb(f"xr{i}", [128, D]) for i in range(2)]
    use_wcache = os.environ.get("BASS_SIM_FP64") != "1"
    NCACHE = 768 if use_wcache else 0
    _wchs = [nc.dram_tensor(f"wch{i}", [128, 128, 2 * SLOT], BF16).ap()
             for i in range((NCACHE + 127) // 128)]

    def wch(idx):
        return _wchs[idx // 128][idx % 128]

    base0 = nc.sbuf_base

    def at(name, shape, dt, off):
        return nc.alloc_sbuf_tensor_at(name, shape, dt,
                                       offset=base0 + off).ap()

    xnp_hi = at("xnp_hi", [128, DC, PGT], BF16, 0)
    xnp_lo = at("xnp_lo", [128, DC, PGT], BF16, 2 * DC * PGT)
    o_ = 4 * DC * PGT
    Kp = at("Kp", [128, PPS, H * DK], F32, o_)
    Ep = at("Ep", [128, PPS, H * M], F32, o_ + 4 * PPS * H * DK)
    Vp = at("Vp", [128, PPS, H * DV], F32, o_ + 4 * PPS * H * (DK + M))
    n_pre = o_ + 4 * PPS * H * (DK + M + DV)
    xn_hi = at("xn_hi", [128, DC, SGT], BF16, 0)
    xn_lo = at("xn_lo", [128, DC, SGT], BF16, 2 * DC * SGT)
    act_hi = at("act_hi", [128, max(FJ, SHGRP), SGT], BF16, 4 * DC * SGT)
    act_lo = at("act_lo", [128, max(FJ, SHGRP), SGT], BF16,
                4 * DC * SGT + 2 * max(FJ, SHGRP) * SGT)
    n_ffn = 4 * DC * SGT + 4 * max(FJ, SHGRP) * SGT
    o_ = 0

    def acc_(n):
        nonlocal o_
        r = o_
        o_ += n
        return r

    Vc2 = [at("Vc0", [128, H * DV], F32, acc_(4 * H * DV))]
    SGc2 = [at("SGc0", [128, H * DV], F32, acc_(4 * H * DV))]
    Ec2 = [at("Ec0", [128, H * M], F32, acc_(4 * H * M))]
    Ktm2 = [at("Ktm0", [128, H * DK], F32, acc_(4 * H * DK))]
    xnc_hi2 = [at("xnc_hi0", [128, DC, 128], BF16, acc_(2 * DC * 128))]
    xnc_lo2 = [at("xnc_lo0", [128, DC, 128], BF16, acc_(2 * DC * 128))]

    Q_tm = at("Q_tm", [128, H * DK], F32, acc_(4 * H * DK))
    Qfm = at("Qfm", [128, KC, 128], F32, acc_(4 * H * DK))
    Kfm = at("Kfm", [128, KC, 128], F32, acc_(4 * H * DK))
    AmaskT2 = [at(f"AmaskT{i}", [128, 128], F32, acc_(512)) for i in range(2)]
    DT2 = [at(f"DT{i}", [128, 128], F32, acc_(512)) for i in range(2)]
    PRT2 = [at(f"PRT{i}", [128, 128], F32, acc_(512)) for i in range(2)]
    EcT2 = [at(f"EcT{i}", [128, 128], F32, acc_(512)) for i in range(2)]
    R2 = [at(f"R{i}", [128, M], F32, acc_(4 * M)) for i in range(2)]
    ok2 = [at(f"ok{i}", [128, M], F32, acc_(4 * M)) for i in range(2)]
    pe2 = [at(f"pe{i}", [128, M], F32, acc_(4 * M)) for i in range(2)]
    PR2 = [at(f"PR{i}", [128, M], F32, acc_(4 * M)) for i in range(2)]
    o12 = [at(f"o1{i}", [128, DV], F32, acc_(4 * DV)) for i in range(2)]
    of2 = [at(f"of{i}", [128, DV], F32, acc_(4 * DV)) for i in range(2)]
    ofm_hi = at("ofm_hi", [128, H, DV], BF16, acc_(2 * H * DV))
    ofm_lo = at("ofm_lo", [128, H, DV], BF16, acc_(2 * H * DV))
    n_att = o_
    Kfm_flat = Kfm.rearrange("p a b -> p (a b)")
    Vc2.append(Vc2[0])
    SGc2.append(SGc2[0])
    Ec2.append(Ec2[0])
    Ktm2.append(Ktm2[0])
    xnc_hi2.append(xnc_hi2[0])
    xnc_lo2.append(xnc_lo2[0])
    n_arena = max(n_pre, n_ffn, n_att)
    assert base0 + n_arena <= 229344, (base0, n_arena)

    pbank = [nc.alloc_psum_tensor(f"pb{i}", [128, 512], F32).ap()
             for i in range(8)]

    P = Prog(nc)
    SEMS = (["mm", "ve", "ac", "gp", "cs"]
            + [f"st{i}" for i in range(NST)]
            + [f"sw{i}" for i in range(NSP)]
            + [f"wc{i}" for i in range(NWC)]
            + ["gx0", "gx1"]
            + [f"so{i}" for i in range(CPS)])
    for s in SEMS:
        P.sem(s)
    pool = Pool(pbank)

    def T(fn, waits=(), inc="mm", skip_chk=False):
        return P.op("tensor", fn, waits, inc)

    def V(fn, waits=(), inc="ve"):
        return P.op("vector", fn, list(waits) + [("ve", P.cnt["ve"], None)], inc)

    def S(fn, waits=(), inc="ac"):
        return P.op("scalar", fn, list(waits) + [("ac", P.cnt["ac"], None)], inc)

    def G(fn, waits=(), inc="gp"):
        return P.op("gpsimd", fn, list(waits) + [("gp", P.cnt["gp"], None)], inc)

    def now(sem, eng=None):
        return (sem, P.cnt[sem], eng)

    class CachedSplitStream:

        def __init__(self):
            self.sti = 0
            self.spi = 0
            self.wci = 0
            self.st_free = [[] for _ in range(NST)]
            self.sp_free = [[] for _ in range(NSP)]
            self.slot_extra = [[] for _ in range(NSP)]
            self.wc_prev = [None] * NWC
            self.cache = {}
            self.nent = 0

        require_cached = False

        def load(self, key, hbm_ap, n):
            assert not self.require_cached or (key is not None
                                               and key in self.cache), key
            di = self.spi % NSP
            self.spi += 1
            slot = spb[di]
            hi = slot[:, 0:n]
            lo = slot[:, SLOT:SLOT + n]
            pair = spb[di].rearrange("p (a b) -> p a b", a=2)[:, :, 0:n]
            if key is not None and key in self.cache:
                idx, ctok = self.cache[key]
                cpair = wch(idx).rearrange("p (a b) -> p a b", a=2)[:, :, 0:n]
                dtok = P.op("sync",
                            lambda E, d=pair, s=cpair:
                            E.dma_start(out=d, in_=s),
                            waits=[self.sp_free[di], ctok],
                            inc=f"sw{di}", amt=16)
                self.sp_free[di] = []
                self.slot_extra[di] = []
                return di, hi, lo, [dtok]
            si = self.sti % NST
            self.sti += 1
            dtok = P.op("sync",
                        lambda E, d=stg[si][:, 0:n], s=hbm_ap:
                        E.dma_start(out=d, in_=s),
                        waits=[self.st_free[si]], inc=f"st{si}", amt=16)
            htok = S(lambda E, h=hi, s=stg[si][:, 0:n]: E.copy(h, s),
                     waits=[dtok] + self.sp_free[di])
            ltok = V(lambda E, l=lo, s=stg[si][:, 0:n], h=hi:
                     E.tensor_tensor(l, s, h, ALU.subtract),
                     waits=[dtok, htok] + self.sp_free[di])
            self.sp_free[di] = []
            self.st_free[si] = [ltok]
            toks = [htok, ltok]
            self.slot_extra[di] = []
            if key is not None and self.nent < NCACHE:
                idx = self.nent
                self.nent += 1
                k2 = self.wci % NWC
                self.wci += 1
                cpair = wch(idx).rearrange("p (a b) -> p a b", a=2)[:, :, 0:n]
                stok = P.op("gpsimd",
                            lambda E, d=cpair, s=pair:
                            E.dma_start(out=d, in_=s),
                            waits=toks + [self.wc_prev[k2]],
                            inc=f"wc{k2}", amt=16)
                self.wc_prev[k2] = stok
                self.cache[key] = (idx, stok)
                self.slot_extra[di] = [stok]
            return di, hi, lo, toks

        def rel(self, di, tok):
            tok = tok if isinstance(tok, list) else [tok]
            self.sp_free[di] = tok + self.slot_extra[di]

    s_w = CachedSplitStream()

    class XStream:
        def __init__(self):
            self.i = 0
            self.free = [[] for _ in range(2)]

        def load(self, hbm_ap):
            i = self.i % 2
            self.i += 1
            tok = P.op("gpsimd",
                       lambda E, d=xr[i], s=hbm_ap: E.dma_start(out=d, in_=s),
                       waits=[self.free[i]], inc=f"gx{i}", amt=16)
            return i, xr[i], tok

        def rel(self, i, tok):
            self.free[i] = tok if isinstance(tok, list) else [tok]

    s_x = XStream()

    def mm3(bank_ap, lh, ll, rh, rl, start, stop, waits=()):
        P.op("tensor", lambda E: E.matmul(bank_ap, lh, rh, start=start,
                                          stop=False), waits, None)
        P.op("tensor", lambda E: E.matmul(bank_ap, lh, rl, start=False,
                                          stop=False), (), None)
        return T(lambda E: E.matmul(bank_ap, ll, rh, start=False, stop=stop))

    for dst, src in ((ident, identH), (utri, utriH), (onesb, onesH),
                     (gnwr, gnwrH), (flag, flagH), (cvals, cvalsH)):
        P.op("gpsimd", lambda E, d=dst, s=src: E.dma_start(out=d, in_=s),
             inc="cs", amt=16)
    const_tok = P.op("gpsimd",
                     lambda E: E.dma_start(
                         out=stg[0][:, 0:DC * (NEXP + 1)],
                         in_=wrt.rearrange("(dc p) c -> p dc c", p=128)),
                     inc="cs", amt=16)
    for e in ENGS:
        P.wait_only(e, const_tok)
    wrt_hi_f = wrt_hi.rearrange("p a b -> p (a b)")
    wrt_lo_f = wrt_lo.rearrange("p a b -> p (a b)")
    h0 = S(lambda E: E.copy(wrt_hi_f, stg[0][:, 0:DC * (NEXP + 1)]),
           waits=[const_tok])
    wrt_tok = V(lambda E: E.tensor_tensor(wrt_lo_f,
                                          stg[0][:, 0:DC * (NEXP + 1)],
                                          wrt_hi_f, ALU.subtract), waits=[h0])
    s_w.st_free[0] = [wrt_tok]

    for a in (U_all.rearrange("p a b -> p (a b)"),
              W_all.rearrange("p a b -> p (a b)"), A_all[:, :]):
        st_tok = G(lambda E, a=a: E.memset(a, 0.0))
    U_ready = [st_tok] * H
    W_ready = [st_tok] * H
    A_ready = [st_tok] * H

    uses = {}

    def use(key, tok):
        uses.setdefault(key, []).append(tok)

    def take(key):
        return uses.pop(key, [])

    def head_slices(h, cp=0):
        qb = 64 * (h % 2)
        return dict(
            Ec=Ec2[cp][:, M * h:M * h + M],
            Vc=Vc2[cp][:, DV * h:DV * h + DV],
            Ktm=Ktm2[cp][:, DK * h:DK * h + DK],
            SG=SGc2[cp][:, DV * h:DV * h + DV],
            Qfm=Qfm[qb:qb + 64, h // 2, :],
            Kfm=Kfm[qb:qb + 64, h // 2, :],
            U=U_all[qb:qb + 64, h // 2, :],
            W=W_all[0:64, h, :],
            A=A_all[0:1, M * h:M * h + M],
            qb=qb,
        )

    def ln_transpose(x_sb, x_tok, dst_hi, dst_lo, key):
        wlist = [x_tok, const_tok] + take(key + "_tm")
        for i in range(nst):
            seg = min(512, D)
            V(lambda E, i=i, seg=seg: E.bn_stats(bnst[:, 6 * i:6 * i + 6],
                                                 x_sb[:, seg * i:seg * (i + 1)]),
              waits=wlist if i == 0 else [])
        V(lambda E: E.bn_aggr(mv2, bnst[:, 0:6 * nst]))
        V(lambda E: E.tensor_scalar(rs1, mv2[:, 1:2], EPS, None, ALU.add))
        V(lambda E: E.reciprocal(rs2, rs1))
        tok_sq = S(lambda E: E.activation(ln_a, rs2, AF.Sqrt),
                   waits=[now("ve", "vector")])
        V(lambda E: E.scalar_tensor_tensor(ln_b, mv2[:, 0:1], -1.0, ln_a,
                                           ALU.mult, ALU.mult),
          waits=[tok_sq])
        last = None
        half = D // NTH
        hc = half // 128
        for th in range(NTH):
            tok_n = S(lambda E, th=th: E.activation(
                xn_tm[:, 0:half], x_sb[:, half * th:half * (th + 1)],
                AF.Identity, bias=ln_b, scale=ln_a),
                waits=[x_tok, now("ve", "vector")] + take(key + "_tm"))
            for g0 in range(0, hc, 4):
                gn = min(4, hc - g0)
                bi, bank, bfree = pool.acq()
                tks = []
                for k in range(gn):
                    tks.append(T(lambda E, k=k, g0=g0, bank=bank:
                                 E.transpose(
                                     bank[:, 128 * k:128 * (k + 1)],
                                     xn_tm[:, 128 * (g0 + k):128 * (g0 + k + 1)],
                                     ident),
                                 waits=[tok_n, bfree, const_tok]))
                gg = th * hc + g0
                hv = S(lambda E, bank=bank, gg=gg, gn=gn:
                       E.copy(dst_hi[:, gg:gg + gn, :],
                              bank[:, 0:128 * gn].rearrange(
                                  "p (a b) -> p a b", a=gn)),
                       waits=[tks[-1]])
                lv = V(lambda E, bank=bank, gg=gg, gn=gn:
                       E.tensor_tensor(dst_lo[:, gg:gg + gn, :],
                                       bank[:, 0:128 * gn].rearrange(
                                           "p (a b) -> p a b", a=gn),
                                       dst_hi[:, gg:gg + gn, :],
                                       ALU.subtract),
                       waits=[tks[-1], hv])
                pool.rel(bi, lv)
                use(key + "_tm", lv)
                last = [hv, lv]
        return last

    def project_tm_sched(srch, srcl, src_tok, specs):
        work = []
        for name, whbm, ncols, evac, prev_uses in specs:
            for cg in range(0, ncols, 512):
                w = min(512, ncols - cg)
                work.append((name, whbm, cg, w, evac, prev_uses))
        return project_tm_sched_groups(srch, srcl, src_tok, work)

    def project_tm_sched_groups(srch, srcl, src_tok, work):
        nsl = DC // SLD
        flat = [(gi, sl) for gi in range(len(work)) for sl in range(nsl)]
        pend = {}

        def preload(ix):
            if ix >= len(flat) or ix in pend:
                return
            gi, sl = flat[ix]
            name, whbm, cg, w, _, _ = work[gi]
            hap = whbm.rearrange("(dc p) c -> p dc c", p=128)[
                :, sl * SLD:(sl + 1) * SLD, cg:cg + w]
            pend[ix] = s_w.load((name, cg, sl), hap, SLD * w)

        def group_thunk(gi):
            name, whbm, cg, w, evac, prev_uses = work[gi]
            if gi == 0:
                preload(0)
                preload(1)
                preload(2)
            bi, bank, bfree = pool.acq()
            first = True
            mt = None
            for sl in range(nsl):
                ix = gi * nsl + sl
                di, hi, lo, stoks = pend.pop(ix)
                preload(ix + 3)
                h4 = hi.rearrange("p (a b) -> p a b", a=SLD)
                l4 = lo.rearrange("p (a b) -> p a b", a=SLD)
                for dd in range(SLD):
                    d = sl * SLD + dd
                    w0 = (stoks + [bfree] + list(src_tok)) if first \
                        else (stoks if dd == 0 else [])
                    mt = mm3(bank[:, 0:w], srch[:, d, :], srcl[:, d, :],
                             h4[:, dd, :], l4[:, dd, :],
                             start=first, stop=(d == DC - 1), waits=w0)
                    first = False
                s_w.rel(di, mt)
            ev = evac(cg, w, bank, mt, prev_uses)
            pool.rel(bi, ev)
            return ev

        return [lambda gi=gi: group_thunk(gi) for gi in range(len(work))]

    def project_tm(srch, srcl, src_tok, specs):
        for t_ in project_tm_sched(srch, srcl, src_tok, specs):
            t_()

    for pg in range(NPG):
        xn_toks = []
        for pc in range(PPS):
            gidx = pg * PPS + pc
            xi, xslot, xtok = s_x.load(xp_t[gidx])
            tk = ln_transpose(xslot, xtok,
                              xnp_hi[:, :, 128 * pc:128 * (pc + 1)],
                              xnp_lo[:, :, 128 * pc:128 * (pc + 1)], "pre")
            s_x.rel(xi, tk)
            xn_toks.append(tk)
        xn_tok = xn_toks[-1]

        kvs_done = {}
        for pc in range(PPS):
            def pv_ev(cg, w, bank, mt, pu, pc=pc):
                ev = V(lambda E, b=bank, cg=cg, w=w, pc=pc:
                       E.tensor_copy(Vp[:, pc, cg:cg + w], b[:, 0:w]),
                       waits=[mt] + pu)
                kvs_done[("v", pc, cg // 512)] = ev
                return ev

            def pk_ev(cg, w, bank, mt, pu, pc=pc):
                ev = V(lambda E, b=bank, cg=cg, w=w, pc=pc:
                       E.tensor_copy(Kp[:, pc, cg:cg + w], b[:, 0:w]),
                       waits=[mt] + pu)
                kvs_done[("k", pc, cg // 512)] = ev
                return ev

            def ps_ev(cg, w, bank, mt, pu, pc=pc):
                cl = V(lambda E, b=bank, w=w: E.tensor_scalar(
                    b[:, 0:w], b[:, 0:w], 32.0, -32.0, ALU.min, ALU.max),
                    waits=[mt] + pu)
                ev = S(lambda E, b=bank, cg=cg, w=w, pc=pc:
                       E.activation(Ep[:, pc, cg:cg + w], b[:, 0:w],
                                    AF.Exp, bias=-32.0), waits=[cl])
                kvs_done[("s", pc, cg // 512)] = ev
                return ev

            project_tm(xnp_hi[:, :, 128 * pc:128 * (pc + 1)],
                       xnp_lo[:, :, 128 * pc:128 * (pc + 1)], xn_tok, [
                           ("wk", wk, H * DK, pk_ev, []),
                           ("wv", wv, H * DV, pv_ev, []),
                           ("ws", ws, H * M, ps_ev, []),
                       ])

        for pc in range(PPS):
            for h in range(H):
                hs = head_slices(h)
                qb = hs["qb"]
                Ep_h = Ep[:, pc, M * h:M * h + M]
                Kp_h = Kp[:, pc, DK * h:DK * h + DK]
                Vp_h = Vp[:, pc, DV * h:DV * h + DV]
                ew = [kvs_done[("s", pc, (M * h) // 512)],
                      kvs_done[("k", pc, (DK * h) // 512)],
                      kvs_done[("v", pc, (DV * h) // 512)]]
                bi, bank, bfree = pool.acq()
                mt = T(lambda E, b=bank, l=Kp_h, r=Ep_h, qb=qb:
                       E.matmul(b[qb:qb + 64, 0:M], l, r, start=True,
                                stop=True),
                       waits=ew + [bfree, U_ready[h]])
                ev = V(lambda E, b=bank, U=hs["U"], qb=qb:
                       E.tensor_tensor(U, U, b[qb:qb + 64, 0:M], ALU.add),
                       waits=[mt, U_ready[h]])
                U_ready[h] = ev
                pool.rel(bi, ev)
                bi, bank, bfree = pool.acq()
                mt = T(lambda E, b=bank, l=Ep_h, r=Vp_h:
                       E.matmul(b[0:64, 0:DV], l, r, start=True, stop=True),
                       waits=ew + [bfree, W_ready[h]])
                ev = V(lambda E, b=bank, W=hs["W"]:
                       E.tensor_tensor(W, W, b[0:64, 0:DV], ALU.add),
                       waits=[mt, W_ready[h]])
                W_ready[h] = ev
                pool.rel(bi, ev)
                bi, bank, bfree = pool.acq()
                T(lambda E, b=bank, r=Ep_h:
                  E.matmul(b[0:1, 0:M], onesb[:, 0:1], r, start=True,
                           stop=False),
                  waits=ew + [bfree, A_ready[h], const_tok])
                mt = T(lambda E, b=bank, A=hs["A"]:
                       E.matmul(b[0:1, 0:M], onesb[0:1, 0:1], A,
                                start=False, stop=True))
                ev = V(lambda E, b=bank, A=hs["A"]:
                       E.tensor_copy(A, b[0:1, 0:M]), waits=[mt, A_ready[h]])
                A_ready[h] = ev
                pool.rel(bi, ev)
        P.barrier()

    fU = V(lambda E: E.tensor_scalar(
        U_all.rearrange("p a b -> p (a b)"),
        U_all.rearrange("p a b -> p (a b)"),
        flag[:, 0:1], None, ALU.mult),
        waits=[U_ready[H - 1], const_tok])
    fW = V(lambda E: E.tensor_scalar(
        W_all.rearrange("p a b -> p (a b)"),
        W_all.rearrange("p a b -> p (a b)"),
        flag[0:64, 0:1], None, ALU.mult), waits=[W_ready[H - 1]])
    fA = V(lambda E: E.tensor_scalar(A_all, A_all, flag[0:1, 0:1], None,
                                     ALU.mult),
           waits=[A_ready[H - 1]])
    U_ready = [fU] * H
    W_ready = [fW] * H
    A_ready = [fA] * H

    res_free = [[] for _ in range(CPS)]
    for sg in range(NSG):
        P.barrier()
        xn_toks = []
        for tt in range(CPS):
            gidx = sg * CPS + tt
            xi, xslot, xtok = s_x.load(xo_t[gidx])
            tk = ln_transpose(xslot, xtok,
                              xn_hi[:, :, 128 * tt:128 * (tt + 1)],
                              xn_lo[:, :, 128 * tt:128 * (tt + 1)], "ffn")
            s_x.rel(xi, tk)
            xn_toks.append(tk)
        xn_tok = xn_toks[-1]

        comb_tok = [None] * CPS
        shg_tok = [None] * CPS
        prev_router_v = None
        for tt in range(CPS):
            bi, bank, bfree = pool.acq()
            mt = None
            for d in range(DC):
                mt = mm3(bank[:, 0:NEXP + 1],
                         xn_hi[:, d, 128 * tt:128 * (tt + 1)],
                         xn_lo[:, d, 128 * tt:128 * (tt + 1)],
                         wrt_hi[:, d, :], wrt_lo[:, d, :],
                         start=(d == 0), stop=(d == DC - 1),
                         waits=(list(xn_toks[tt]) + [wrt_tok, bfree])
                         if d == 0 else [])
            V(lambda E, b=bank: E.reduce_max(sm_m, b[:, 0:NEXP], AX.X,
                                             negate=True), waits=[mt])
            t_pe = S(lambda E, b=bank: E.activation(per, b[:, 0:NEXP], AF.Exp,
                                                    bias=sm_m,
                                                    accum_out=sm_s),
                     waits=[mt, now("ve", "vector"), prev_router_v])
            t_sg = S(lambda E, b=bank, tt=tt:
                     E.activation(shg[:, tt:tt + 1], b[:, NEXP:NEXP + 1],
                                  AF.Sigmoid), waits=[mt])
            shg_tok[tt] = t_sg
            V(lambda E: E.reciprocal(sm_r, sm_s), waits=[t_pe])
            t_pr = V(lambda E: E.tensor_scalar(probs, per, sm_r, None,
                                               ALU.mult))
            pool.rel(bi, [t_pr, t_sg])
            V(lambda E: E.tensor_copy(cur, probs))
            for j in range(TOPK):
                V(lambda E: E.reduce_max(sm_m, cur, AX.X))
                V(lambda E: E.tensor_scalar(eqb, cur, sm_m, None,
                                            ALU.is_equal))
                if j == 0:
                    V(lambda E: E.tensor_copy(sel, eqb))
                else:
                    V(lambda E: E.tensor_tensor(sel, sel, eqb, ALU.add))
                if j < TOPK - 1:
                    V(lambda E: E.scalar_tensor_tensor(cur, eqb, -1e30, cur,
                                                       ALU.mult, ALU.add))
            V(lambda E: E.tensor_tensor(w4, probs, sel, ALU.mult))
            V(lambda E: E.reduce_sum(sm_s, w4, AX.X))
            V(lambda E: E.reciprocal(sm_r, sm_s))
            comb_tok[tt] = V(lambda E, tt=tt: E.tensor_scalar(
                comb[:, tt, :], w4, sm_r, None, ALU.mult))
            prev_router_v = comb_tok[tt]

        res_written = [[False] * NDS for _ in range(CPS)]

        def gu_phase(wg_h, wu_h, nj, col0, kg, ku):
            a_toks = []
            prev_hi = None
            prev_lo = None
            pend = {}

            def preload(j):
                if j >= nj:
                    return
                pend[j] = (s_w.load((kg, col0 + 128 * j),
                                    colslice(wg_h, col0 + 128 * j), DC * 128),
                           s_w.load((ku, col0 + 128 * j),
                                    colslice(wu_h, col0 + 128 * j), DC * 128))

            preload(0)
            preload(1)
            for j in range(nj):
                (dg, ghi, glo, tg), (du, uhi, ulo, tu) = pend.pop(j)
                preload(j + 2)
                gh = ghi.rearrange("p (a b) -> p a b", a=DC)
                gl = glo.rearrange("p (a b) -> p a b", a=DC)
                uh = uhi.rearrange("p (a b) -> p a b", a=DC)
                ul = ulo.rearrange("p (a b) -> p a b", a=DC)
                bg, bankg, bfg = pool.acq()
                mg = None
                for d in range(DC):
                    mg = mm3(bankg[:, 0:SGT], gh[:, d, :], gl[:, d, :],
                             xn_hi[:, d, :], xn_lo[:, d, :],
                             start=(d == 0), stop=(d == DC - 1),
                             waits=(tg + list(xn_tok) + [bfg])
                             if d == 0 else [])
                s_w.rel(dg, mg)
                bu, banku, bfu = pool.acq()
                mu = None
                for d in range(DC):
                    mu = mm3(banku[:, 0:SGT], uh[:, d, :], ul[:, d, :],
                             xn_hi[:, d, :], xn_lo[:, d, :],
                             start=(d == 0), stop=(d == DC - 1),
                             waits=(tu + [bfu]) if d == 0 else [])
                s_w.rel(du, mu)
                t_sil = S(lambda E, b=bankg: E.activation(silu_t, b[:, 0:SGT],
                                                          AF.Silu),
                          waits=[mg, prev_hi, prev_lo])
                pool.rel(bg, t_sil)
                tA = V(lambda E, b=banku: E.tensor_tensor(
                    silu_t, silu_t, b[:, 0:SGT], ALU.mult),
                    waits=[mu, t_sil, prev_lo])
                hv = S(lambda E, j=j: E.copy(act_hi[:, j, :], silu_t),
                       waits=[tA] + take(f"act{j}"))
                lv = V(lambda E, j=j: E.tensor_tensor(
                    act_lo[:, j, :], silu_t, act_hi[:, j, :], ALU.subtract),
                    waits=[hv])
                pool.rel(bu, lv)
                prev_hi = hv
                prev_lo = lv
                a_toks.append([hv, lv])
            return a_toks

        def down_phase(get_slice, nj, scal_for_tt, sc_tok, act_tok,
                       first_write, kdown):
            JG = max(1, min(nj, SLOT // DSEG))
            pend = {}

            def preload(ds):
                if ds >= NDS:
                    return
                sl = []
                for j0 in range(0, nj, JG):
                    jn = min(JG, nj - j0)
                    di, whi, wlo, stoks = s_w.load((kdown, ds, j0),
                                                   get_slice(ds, j0, jn),
                                                   jn * DSEG)
                    sl.append((di,
                               whi.rearrange("p (a b) -> p a b", a=jn),
                               wlo.rearrange("p (a b) -> p a b", a=jn),
                               stoks, j0, jn))
                pend[ds] = sl

            preload(0)
            for ds in range(NDS):
                slots = pend.pop(ds)
                preload(ds + 1)
                last_mt = None
                for tt in range(CPS):
                    bi, bank, bfree = pool.acq()
                    mt = None
                    first = True
                    for (di, wh, wl, stoks, j0, jn) in slots:
                        for jj in range(jn):
                            j = j0 + jj
                            w0 = []
                            if first:
                                w0 = (stoks + [bfree] + act_tok
                                      + res_free[tt])
                            elif jj == 0:
                                w0 = list(stoks)
                            mt = mm3(bank[:, 0:DSEG],
                                     act_hi[:, j, 128 * tt:128 * (tt + 1)],
                                     act_lo[:, j, 128 * tt:128 * (tt + 1)],
                                     wh[:, jj, :], wl[:, jj, :],
                                     start=first, stop=(j == nj - 1),
                                     waits=w0)
                            first = False
                    last_mt = mt
                    sc = scal_for_tt(tt)
                    dst = res[:, tt, ds * DSEG:(ds + 1) * DSEG]
                    if first_write and not res_written[tt][ds]:
                        ev = V(lambda E, b=bank, dst=dst, sc=sc:
                               E.tensor_scalar(dst, b[:, 0:DSEG], sc, None,
                                               ALU.mult),
                               waits=[mt, sc_tok] + res_free[tt])
                        res_written[tt][ds] = True
                    else:
                        ev = V(lambda E, b=bank, dst=dst, sc=sc:
                               E.scalar_tensor_tensor(dst, b[:, 0:DSEG], sc,
                                                      dst, ALU.mult, ALU.add),
                               waits=[mt, sc_tok])
                    pool.rel(bi, ev)
                    use(f"res{tt}", ev)
                for (di, _, _, _, _, _) in slots:
                    s_w.rel(di, last_mt)

        for e in range(NEXP):
            a_toks = gu_phase(wgate[e], wup[e], FJ, 0, ("g", e), ("u", e))
            down_phase(lambda ds, j0, jn, e=e: wd_slice(wdown[e], ds, j0, jn),
                       FJ, lambda tt, e=e: comb[:, tt, e:e + 1],
                       comb_tok[CPS - 1], a_toks[-1], first_write=(e == 0),
                       kdown=("d", e))
            dn = now("mm", "tensor")
            for j in range(FJ):
                use(f"act{j}", dn)

        j0 = 0
        while j0 < SHJ:
            ng = min(SHGRP, SHJ - j0)
            a_toks = gu_phase(wsg, wsu, ng, 128 * j0, "sg", "su")
            down_phase(lambda ds, jj0, jn, j0=j0:
                       wd_slice(wsd, ds, j0 + jj0, jn), ng,
                       lambda tt: shg[:, tt:tt + 1],
                       shg_tok[CPS - 1], a_toks[-1], first_write=False,
                       kdown=("sd", j0))
            dn = now("mm", "tensor")
            for j in range(ng):
                use(f"act{j}", dn)
            j0 += ng

        P.barrier()
        for ci in range(CPS):
            gc = sg * CPS + ci
            cp = 0
            xnch, xncl = xnc_hi2[cp], xnc_lo2[cp]
            VcP, SGcP, EcP, KtmP = Vc2[cp], SGc2[cp], Ec2[cp], Ktm2[cp]
            xi, x_sb, xtok = s_x.load(xo_t[gc])
            tk_xn = ln_transpose(x_sb, xtok, xnch, xncl, "att")

            pV = take(f"Vc{cp}"); pE = take(f"Ec{cp}"); pG = take(f"SGc{cp}")
            pQ = take("Qtm"); pK = take("Ktm0")

            def ev_v(cg, w, bank, mt, pu, dst=VcP):
                return V(lambda E, b=bank, cg=cg, w=w, dst=dst:
                         E.tensor_copy(dst[:, cg:cg + w], b[:, 0:w]),
                         waits=[mt] + pu)

            def ev_s(cg, w, bank, mt, pu, dst=EcP):
                cl = V(lambda E, b=bank, w=w: E.tensor_scalar(
                    b[:, 0:w], b[:, 0:w], 32.0, -32.0, ALU.min, ALU.max),
                    waits=[mt] + pu)
                return S(lambda E, b=bank, cg=cg, w=w, dst=dst:
                         E.activation(dst[:, cg:cg + w], b[:, 0:w],
                                      AF.Exp, bias=-32.0), waits=[cl])

            def ev_g(cg, w, bank, mt, pu, dst=SGcP):
                return S(lambda E, b=bank, cg=cg, w=w, dst=dst:
                         E.activation(dst[:, cg:cg + w], b[:, 0:w],
                                      AF.Silu), waits=[mt] + pu)

            def ev_q(cg, w, bank, mt, pu):
                return S(lambda E, b=bank, cg=cg, w=w:
                         E.mul(Q_tm[:, cg:cg + w], b[:, 0:w], 0.125),
                         waits=[mt] + pu)

            def ev_k(cg, w, bank, mt, pu, dst=KtmP):
                return V(lambda E, b=bank, cg=cg, w=w, dst=dst:
                         E.tensor_copy(dst[:, cg:cg + w], b[:, 0:w]),
                         waits=[mt] + pu)

            q_toks = {}; k_toks = {}; s_toks = {}

            def ev_q3(cg, w, bank, mt, pu):
                t = ev_q(cg, w, bank, mt, pu); q_toks[cg // 512] = t; return t

            def ev_k3(cg, w, bank, mt, pu):
                t = ev_k(cg, w, bank, mt, pu); k_toks[cg // 512] = t; return t

            def ev_s3(cg, w, bank, mt, pu):
                t = ev_s(cg, w, bank, mt, pu); s_toks[cg // 512] = t; return t

            project_tm(xnch, xncl, tk_xn, [
                ("wq", wq, H * DK, ev_q3, pQ),
                ("wk", wk, H * DK, ev_k3, pK),
                ("ws", ws, H * M, ev_s3, pE),
            ])

            qfm_ev = {}; kfm_ev = {}
            for cc in range(KC):
                bi, bank, bfree = pool.acq()
                tk = T(lambda E, b=bank, cc=cc:
                       E.transpose(b[:, 0:128],
                                   Q_tm[:, 128 * cc:128 * (cc + 1)], ident),
                       waits=[q_toks[(128 * cc) // 512], bfree])
                ev = V(lambda E, b=bank, cc=cc:
                       E.tensor_copy(Qfm[:, cc, :], b[:, 0:128]),
                       waits=[tk] + take(f"Qfm{cc}"))
                pool.rel(bi, ev)
                use("Qtm", tk)
                qfm_ev[cc] = ev
                bi, bank, bfree = pool.acq()
                tk = T(lambda E, b=bank, cc=cc, src_=KtmP:
                       E.transpose(b[:, 0:128],
                                   src_[:, 128 * cc:128 * (cc + 1)], ident),
                       waits=[k_toks[(128 * cc) // 512], bfree])
                ev = V(lambda E, b=bank, cc=cc:
                       E.tensor_copy(Kfm[:, cc, :], b[:, 0:128]),
                       waits=[tk] + take(f"Kfm{cc}"))
                pool.rel(bi, ev)
                use("Ktm0", tk)
                kfm_ev[cc] = ev

            vc_toks = {}
            sg_toks = {}

            def ev_v2(cg, w, bank, mt, pu):
                t = ev_v(cg, w, bank, mt, pu)
                vc_toks[cg // 512] = t
                return t

            def ev_g2(cg, w, bank, mt, pu):
                t = ev_g(cg, w, bank, mt, pu)
                sg_toks[cg // 512] = t
                return t

            vg_work = []
            nvg = (H * DV + 511) // 512
            for gidx2 in range(nvg):
                cg0 = 512 * gidx2
                w0_ = min(512, H * DV - cg0)
                vg_work.append(("wv", wv, cg0, w0_, ev_v2, pV if gidx2 == 0 else []))
                vg_work.append(("wg", wg, cg0, w0_, ev_g2, pG if gidx2 == 0 else []))
            vg_thunks = project_tm_sched_groups(xnch, xncl, tk_xn, vg_work)

            hst = {}

            def stA(h):
                p = h % 2
                hs = head_slices(h, cp)
                d = {"hs": hs}
                A_h = hs["A"]; Ec_h = hs["Ec"]
                biA, bA, bfA = pool.acq()
                T(lambda E, b=bA, A=A_h: E.matmul(b[:, 0:M],
                                                  onesb[0:1, 0:128],
                                                  A, start=True, stop=False),
                  waits=[A_ready[h], bfA, const_tok])
                T(lambda E, b=bA, Eh=Ec_h: E.matmul(
                    b[:, 0:M], utri, Eh, start=False, stop=True),
                  waits=[s_toks[(M * h) // 512]])
                T(lambda E, b=bA, Eh=Ec_h: E.matmul(
                    b[0:1, M:2 * M], onesb[:, 0:1], Eh, start=True,
                    stop=False, skip_group_check=True))
                tokN = T(lambda E, b=bA, A=A_h: E.matmul(
                    b[0:1, M:2 * M], onesb[0:1, 0:1], A, start=False,
                    stop=True, skip_group_check=True))
                biM, bM, bfM = pool.acq()
                tokAm = T(lambda E, b=bM, K=hs["Kfm"], Q=hs["Qfm"]:
                          E.matmul(b[:, 0:128], K, Q, start=True, stop=True),
                          waits=[bfM, qfm_ev[h // 2], kfm_ev[h // 2]])
                d["tokR"] = V(lambda E, b=bA, p=p: E.reciprocal(
                    R2[p][:, 0:M], b[:, 0:M]), waits=[tokN])
                tokAh = V(lambda E, b=bA, A=A_h: E.tensor_copy(
                    A, b[0:1, M:2 * M]), waits=[tokN, A_ready[h]])
                A_ready[h] = tokAh
                pool.rel(biA, [d["tokR"], tokAh])
                d["tokAmm"] = V(lambda E, b=bM, p=p: E.tensor_tensor(
                    AmaskT2[p], b[:, 0:128], utri, ALU.mult), waits=[tokAm])
                pool.rel(biM, d["tokAmm"])
                hst[h] = d

            def stB(h):
                p = h % 2
                d = hst[h]; hs = d["hs"]
                Ec_h = hs["Ec"]
                biG, bG, bfG = pool.acq()
                T(lambda E, b=bG, Q=hs["Qfm"], U=hs["U"]:
                  E.matmul(b[:, 0:M], Q, U, start=True, stop=False),
                  waits=[U_ready[h], bfG])
                T(lambda E, b=bG, Eh=Ec_h, p=p:
                  E.matmul(b[:, 0:M], AmaskT2[p], Eh, start=False,
                           stop=True),
                  waits=[d["tokAmm"]])
                bU = bG
                tokUm = T(lambda E, b=bU, K=hs["Ktm"], Eh=Ec_h, qb=hs["qb"]:
                          E.matmul(b[qb:qb + 64, M:M + M], K, Eh, start=True,
                                   stop=True, skip_group_check=True),
                          waits=[k_toks[(DK * h) // 512]])
                tokok = V(lambda E, b=bG, p=p: E.tensor_tensor(
                    ok2[p][:, 0:M], b[:, 0:M], R2[p][:, 0:M], ALU.mult),
                    waits=[tokUm, d["tokR"]])
                V(lambda E, p=p: E.reduce_max(cm_m[p], ok2[p][:, 0:M], AX.X,
                                              negate=True))
                tokpe = S(lambda E, p=p: E.activation(
                    pe2[p][:, 0:M], ok2[p][:, 0:M], AF.Exp, bias=cm_m[p],
                    accum_out=cm_s[p]), waits=[now("ve", "vector")])
                V(lambda E, p=p: E.reciprocal(cm_r[p], cm_s[p]),
                  waits=[tokpe])
                d["tokPR"] = V(lambda E, p=p: E.scalar_tensor_tensor(
                    PR2[p][:, 0:M], pe2[p][:, 0:M], cm_r[p], R2[p][:, 0:M],
                    ALU.mult, ALU.mult))
                ev = V(lambda E, b=bU, U=hs["U"], qb=hs["qb"]:
                       E.tensor_tensor(U, U, b[qb:qb + 64, M:M + M], ALU.add),
                       waits=[tokUm])
                U_ready[h] = ev
                pool.rel(biG, [tokok, ev])

            def stC(h):
                p = h % 2
                d = hst[h]; hs = d["hs"]
                biP, bP, bfP = pool.acq()
                tkP = T(lambda E, b=bP, p=p: E.transpose(
                    b[0:M, 0:128], PR2[p][:, 0:M], ident),
                    waits=[d["tokPR"], bfP])
                evP = V(lambda E, b=bP, p=p: E.tensor_copy(
                    PRT2[p][0:M, :], b[0:M, 0:128]), waits=[tkP])
                pool.rel(biP, evP)
                biE, bE, bfE = pool.acq()
                tkE = T(lambda E, b=bE, Eh=hs["Ec"]: E.transpose(
                    b[0:M, 0:128], Eh, ident), waits=[bfE])
                evE = V(lambda E, b=bE, p=p: E.tensor_copy(
                    EcT2[p][0:M, :], b[0:M, 0:128]), waits=[tkE])
                pool.rel(biE, evE)
                biD, bD, bfD = pool.acq()
                tokD = T(lambda E, b=bD, p=p: E.matmul(
                    b[:, 0:128], EcT2[p][0:M, :], PRT2[p][0:M, :],
                    start=True, stop=True),
                    waits=[evP, evE, bfD])
                d["tokDm"] = V(lambda E, b=bD, p=p: E.tensor_tensor(
                    DT2[p], b[:, 0:128], utri, ALU.mult), waits=[tokD])
                pool.rel(biD, d["tokDm"])

            def stD(h):
                p = h % 2
                d = hst[h]; hs = d["hs"]
                biO, bO, bfO = pool.acq()
                gi4 = (DV * h) // 512
                T(lambda E, b=bO, Vh=hs["Vc"], p=p: E.matmul(
                    b[:, 0:DV], DT2[p], Vh, start=True, stop=False),
                    waits=[d["tokDm"], bfO, vc_toks[gi4], sg_toks[gi4]])
                tokO = T(lambda E, b=bO, W=hs["W"], p=p: E.matmul(
                    b[:, 0:DV], PRT2[p][0:M, :], W, start=False, stop=True),
                    waits=[W_ready[h]])
                bW = bO
                tokWm = T(lambda E, b=bW, Eh=hs["Ec"], Vh=hs["Vc"]:
                          E.matmul(b[0:64, DV:2 * DV], Eh, Vh, start=True,
                                   stop=True, skip_group_check=True))
                tokSq = S(lambda E, b=bO, p=p: E.activation(
                    sqd[p], b[:, 0:DV], AF.Square, accum_out=cr1[p]),
                    waits=[tokWm])
                V(lambda E, p=p: E.tensor_scalar(cr2[p], cr1[p], 1.0 / DV,
                                                 EPS, ALU.mult, ALU.add),
                  waits=[tokSq])
                V(lambda E, p=p: E.reciprocal(cr1[p], cr2[p]))
                tokRs = S(lambda E, p=p: E.activation(crs[p], cr1[p],
                                                      AF.Sqrt),
                          waits=[now("ve", "vector")])
                toko1 = V(lambda E, b=bO, p=p: E.scalar_tensor_tensor(
                    o12[p][:, 0:DV], b[:, 0:DV], crs[p], gnwr, ALU.mult,
                    ALU.mult),
                    waits=[tokRs, const_tok])
                d["tokof"] = V(lambda E, p=p, SG=hs["SG"]: E.tensor_tensor(
                    of2[p][:, 0:DV], o12[p][:, 0:DV], SG, ALU.mult))
                evW = V(lambda E, b=bW, W=hs["W"]:
                        E.tensor_tensor(W, W, b[0:64, DV:2 * DV], ALU.add),
                        waits=[tokWm])
                W_ready[h] = evW
                pool.rel(biO, [toko1, evW])

            def stE(h):
                p = h % 2
                d = hst[h]
                biT, bT, bfT = pool.acq()
                tkT = T(lambda E, b=bT, p=p: E.transpose(
                    b[:, 0:DV], of2[p][:, 0:DV], ident),
                    waits=[d["tokof"], bfT])
                hvT = S(lambda E, b=bT, h=h: E.copy(ofm_hi[:, h, :],
                                                    b[:, 0:DV]),
                        waits=[tkT] + take(f"ofm{h}"))
                lvT = V(lambda E, b=bT, h=h: E.tensor_tensor(
                    ofm_lo[:, h, :], b[:, 0:DV], ofm_hi[:, h, :],
                    ALU.subtract),
                    waits=[hvT])
                pool.rel(biT, lvT)
                use(f"ofm_rd{h}", hvT)
                use(f"ofm_rd{h}", lvT)

            vg_i = 0
            for h0 in range(0, H, 2):
                need = ((min(h0 + 1, H - 1)) * DV) // 512 + 1
                while vg_i < 2 * need and vg_i < len(vg_thunks):
                    vg_thunks[vg_i]()
                    vg_i += 1
                for st in (stA, stB, stC, stD, stE):
                    st(h0)
                    if h0 + 1 < H:
                        st(h0 + 1)
            while vg_i < len(vg_thunks):
                vg_thunks[vg_i]()
                vg_i += 1

            core_mm = now("mm", "tensor")
            core_ve = now("ve", "vector")
            use(f"Vc{cp}", core_mm)
            use(f"SGc{cp}", core_ve)
            use(f"Ec{cp}", core_mm)
            use("Ktm0", core_mm)
            for cc in range(KC):
                use(f"Qfm{cc}", core_mm)
                use(f"Kfm{cc}", core_mm)

            ofm_toks = take(f"ofm_rd{H - 1}")
            for h in range(H - 1):
                take(f"ofm_rd{h}")
            nseg = max(D // 512, 1)
            segw = min(512, D)
            wo_banks = [pool.acq() for _ in range(nseg)]
            last_mts = [None] * nseg
            for v in range(VC):
                di, whi, wlo, stoks = s_w.load(("wo", v), wo_t[v], D)
                for dsg in range(nseg):
                    bi, bank, bfree = wo_banks[dsg]
                    w0 = (stoks + ofm_toks + res_free[ci]
                          + (bfree if v == 0 else []))
                    last_mts[dsg] = mm3(
                        bank[:, 0:segw],
                        ofm_hi[:, v, :], ofm_lo[:, v, :],
                        whi[:, segw * dsg:segw * (dsg + 1)],
                        wlo[:, segw * dsg:segw * (dsg + 1)],
                        start=(v == 0), stop=(v == VC - 1),
                        waits=w0 if v == 0 else stoks)
                s_w.rel(di, last_mts[-1])
            for h in range(H):
                use(f"ofm{h}", last_mts[-1])
            fl_toks = []
            for dsg in range(nseg):
                bi, bank, bfree = wo_banks[dsg]
                t1 = V(lambda E, b=bank, dsg=dsg, x=x_sb, segw=segw:
                       E.tensor_tensor(af32[:, 0:segw], b[:, 0:segw],
                                       x[:, segw * dsg:segw * (dsg + 1)],
                                       ALU.add),
                       waits=[last_mts[dsg], xtok] + take(f"res{ci}")
                       + res_free[ci])
                t2 = V(lambda E, dsg=dsg, ci=ci, segw=segw:
                       E.tensor_tensor(res[:, ci, segw * dsg:segw * (dsg + 1)],
                                       res[:, ci, segw * dsg:segw * (dsg + 1)],
                                       af32[:, 0:segw], ALU.add))
                pool.rel(bi, t2)
                fl_toks.append(t2)
            s_x.rel(xi, fl_toks[-1])
            otok = P.op("gpsimd",
                        lambda E, ci=ci, gc=gc: E.dma_start(out=out_t[gc],
                                                            in_=res[:, ci, :]),
                        waits=fl_toks, inc=f"so{ci}", amt=16)
            res_free[ci] = [otok]
        P.barrier()

    for ci in range(CPS):
        P.wait_only("gpsimd", (f"so{ci}", P.cnt[f"so{ci}"], None))

    import contextlib
    with nc.Block() as block:
        with contextlib.ExitStack() as stack:
            P.sems = {s: stack.enter_context(nc.semaphore(f"s_{s}"))
                      for s in SEMS}

            @block.tensor
            def _(eng):
                for f in P.stream["tensor"]:
                    f(eng)

            @block.vector
            def _(eng):
                for f in P.stream["vector"]:
                    f(eng)

            @block.scalar
            def _(eng):
                for f in P.stream["scalar"]:
                    f(eng)

            @block.gpsimd
            def _(eng):
                for f in P.stream["gpsimd"]:
                    f(eng)

            @block.sync
            def _(eng):
                for f in P.stream["sync"]:
                    f(eng)

    return nc


def make_inputs_per_core(cfg, inputs):
    NC = cfg["NC"]; TO = cfg["TO"]; TP = cfg["TP"]; DV = cfg["DV"]
    x = np.ascontiguousarray(np.asarray(inputs["L0_attn_hidden_states"],
                                        np.float32))
    B = x.shape[0]
    halves = max(NC // B, 1)

    def cc(a):
        return np.ascontiguousarray(np.asarray(a, np.float32))

    shared = dict(
        wq=cc(inputs["L0_attn_Wq"]), wk=cc(inputs["L0_attn_Wk"]),
        wv=cc(inputs["L0_attn_Wv"]), ws=cc(inputs["L0_attn_Ws"]),
        wg=cc(inputs["L0_attn_Wg"]), wo=cc(inputs["L0_attn_Wo"]),
        wrt=np.ascontiguousarray(np.concatenate(
            [np.asarray(inputs["L0_ffn_w_router"], np.float32),
             np.asarray(inputs["L0_ffn_w_sh_expert_gate"], np.float32)],
            axis=1)),
        wgate=cc(inputs["L0_ffn_w_gate"]), wup=cc(inputs["L0_ffn_w_up"]),
        wdown=cc(inputs["L0_ffn_w_down"]),
        wsg=cc(inputs["L0_ffn_w_sh_gate"]), wsu=cc(inputs["L0_ffn_w_sh_up"]),
        wsd=cc(inputs["L0_ffn_w_sh_down"]),
        identH=np.eye(128, dtype=np.float32),
        utriH=np.ascontiguousarray(np.triu(np.ones((128, 128), np.float32))),
        onesH=np.ones((128, 128), np.float32),
        gnwrH=np.ascontiguousarray(np.tile(
            np.asarray(inputs["L0_attn_g_norm_weight"], np.float32)[None, :DV],
            (128, 1))),
        cvalsH=np.ascontiguousarray(np.tile(
            np.array([[0.0, -32.0, 1.0, 0.125]], np.float32), (128, 1))),
    )
    in_maps = []
    for c in range(NC):
        b = c // halves
        half = c % halves
        t0 = half * TO
        xo = np.ascontiguousarray(x[b, t0:t0 + TO, :])
        if half > 0:
            xpre = np.ascontiguousarray(x[b, t0 - TP:t0, :])
            fl = 1.0
        else:
            xpre = xo
            fl = 0.0
        m = dict(shared)
        m["xo"] = xo
        m["xp"] = xpre
        m["flagH"] = np.full((128, 1), fl, np.float32)
        in_maps.append(m)
    return in_maps


def kernel(**inputs):
    cfg = CFG_FULL
    nc = build(cfg)
    in_maps = make_inputs_per_core(cfg, inputs)
    res = run_bass_kernel_spmd(nc, in_maps, core_ids=list(range(cfg["NC"])))
    x = np.asarray(inputs["L0_attn_hidden_states"])
    B, Tt, Dd = x.shape
    out = np.empty((B, Tt, Dd), np.float32)
    halves = max(cfg["NC"] // B, 1)
    for c in range(cfg["NC"]):
        b = c // halves
        t0 = (c % halves) * cfg["TO"]
        out[b, t0:t0 + cfg["TO"], :] = res.results[c]["out"]
    return out
